# version 13
# baseline (speedup 1.0000x reference)
import numpy as np
import ml_dtypes
from contextlib import ExitStack

import concourse.bass as bass
import concourse.mybir as mybir
from concourse.bass import IndirectOffsetOnAxis
from concourse.bass_utils import run_bass_kernel_spmd

F32 = mybir.dt.float32
BF16 = mybir.dt.bfloat16
I32 = mybir.dt.int32
AF = mybir.ActivationFunctionType
ALU = mybir.AluOpType
AX = mybir.AxisListType

P = 128
D = 2048
S = 4096
NH = 16
HD = 128
ND = 16
OWN = 1024
NT = 8
NKA = 12
NKB = 28
NIE = 11
NIS = 44
NE = 16
QSCALE = float(1.0 / np.sqrt(128.0))
EPS = 1e-5

ENGS = ("tensor", "scalar", "vector", "sync", "gpsimd")
STREAM_SEM = {"tensor": "pe", "scalar": "act", "vector": "dve", "gpsimd": "pool"}


class Ctx:

    def __init__(self, nc, stack):
        self.nc = nc
        self.stack = stack
        self.ops = {e: [] for e in ENGS}
        self.cnt = {}
        self.sem = {}
        self.lastw = {}
        self.readers = {}
        self.waited = {e: {} for e in ENGS}
        self.dsems = set()
        for s in ("pe", "act", "dve", "pool"):
            self._mksem(s)

    def _mksem(self, name):
        if name not in self.sem:
            self.sem[name] = self.stack.enter_context(self.nc.semaphore(name))
            self.cnt[name] = 0
        return name

    def emit(self, eng, build, reads=(), writes=(), dma=False, pe_inc=True,
             dsem=None):
        if dma:
            key = dsem if dsem is not None else "D" + repr(writes[0])
            semname = self._mksem(key)
            self.dsems.add(semname)
            n = 16
            inc = True
        else:
            semname = STREAM_SEM[eng]
            n = 1
            inc = True
            if eng == "tensor":
                inc = bool(pe_inc)

        need = {}

        def _need(tok, waw_dma=False):
            if tok is None:
                return
            s, c = tok
            if s == "pe" and eng == "tensor" and not dma:
                return
            if waw_dma and s in self.dsems:
                return
            if c > self.cnt[s]:
                raise RuntimeError(
                    f"wait on future inc: {s}>={c} (emitted {self.cnt[s]}); "
                    f"a PE-read buffer is being overwritten before its "
                    f"accumulation group end was emitted")
            if need.get(s, 0) < c:
                need[s] = c

        for r in reads:
            for s, c in self.lastw.get(r, {}).items():
                _need((s, c))
        for w in writes:
            for s, c in self.lastw.get(w, {}).items():
                _need((s, c), waw_dma=dma)
            for s, c in self.readers.get(w, {}).items():
                _need((s, c))

        waits = []
        wd = self.waited[eng]
        for s, c in need.items():
            if wd.get(s, 0) < c:
                waits.append((s, c))
                wd[s] = c

        if inc:
            self.cnt[semname] += n
            tok = (semname, self.cnt[semname])
        else:
            tok = ("pe", self.cnt["pe"] + 1)

        for r in reads:
            d = self.readers.setdefault(r, {})
            if d.get(tok[0], 0) < tok[1]:
                d[tok[0]] = tok[1]
        for w in writes:
            d = self.lastw.setdefault(w, {})
            if d.get(tok[0], 0) < tok[1]:
                d[tok[0]] = tok[1]
            self.readers[w] = {}

        sem = self.sem
        semh = sem[semname]

        def closure(e, waits=waits, build=build, inc=inc, semh=semh, n=n):
            for s, c in waits:
                e.wait_ge(sem[s], c)
            ins = build(e)
            if inc:
                ins.then_inc(semh, n)

        self.ops[eng].append(closure)

    def barrier(self):
        snap = dict(self.cnt)
        sem = self.sem
        for eng in ENGS:
            wd = self.waited[eng]
            waits = []
            for s, c in snap.items():
                if c > 0 and wd.get(s, 0) < c:
                    waits.append((s, c))
                    wd[s] = c
            if waits:
                def closure(e, waits=waits):
                    for s, c in waits:
                        e.wait_ge(sem[s], c)
                self.ops[eng].append(closure)

    def final_waits(self):
        snap = dict(self.cnt)
        sem = self.sem

        def closure(e):
            for s, c in snap.items():
                if c > 0:
                    e.wait_ge(sem[s], c)
        self.ops["sync"].append(closure)

    def replay(self, block):
        for eng in ENGS:
            lst = self.ops[eng]
            if not lst:
                continue
            def runner(e, lst=lst):
                for f in lst:
                    f(e)
            getattr(block, eng)(runner)


def build_program():
    nc = bass.Bass("TRN2", target_bir_lowering=False)

    xfull = nc.dram_tensor("xfull", [S, D], F32, kind="ExternalInput")
    xown = nc.dram_tensor("xown", [OWN, D], F32, kind="ExternalInput")
    btab = nc.dram_tensor("btab", [P, 64], F32, kind="ExternalInput")
    kvidx = nc.dram_tensor("kvidx", [P, 10], I32, kind="ExternalInput")
    qwt = nc.dram_tensor("qwt", [D, D], F32, kind="ExternalInput")
    kwt = nc.dram_tensor("kwt", [D, D], F32, kind="ExternalInput")
    vwt = nc.dram_tensor("vwt", [D, D], F32, kind="ExternalInput")
    owt = nc.dram_tensor("owt", [D, D], F32, kind="ExternalInput")
    gwt = nc.dram_tensor("gwt", [HD, HD], F32, kind="ExternalInput")
    gateb = nc.dram_tensor("gateb", [P, 1], F32, kind="ExternalInput")
    gwsg = nc.dram_tensor("gwsg", [D, 17], F32, kind="ExternalInput")
    sw1 = nc.dram_tensor("sw1", [D, 5632], F32, kind="ExternalInput")
    sw3 = nc.dram_tensor("sw3", [D, 5632], F32, kind="ExternalInput")
    sw2 = nc.dram_tensor("sw2", [5632, D], F32, kind="ExternalInput")
    w1 = nc.dram_tensor("w1", [NE * D, 1408], F32, kind="ExternalInput")
    w3 = nc.dram_tensor("w3", [NE * D, 1408], F32, kind="ExternalInput")
    w2 = nc.dram_tensor("w2", [NE * 1408, D], F32, kind="ExternalInput")
    out = nc.dram_tensor("out", [OWN, D], F32, kind="ExternalOutput")

    ln0st = nc.dram_tensor("ln0st", [S, D], BF16, kind="Internal")
    lnfm = nc.dram_tensor("lnfm", [ND, P, S], BF16, kind="Internal")
    ownst = nc.dram_tensor("ownst", [OWN, D], BF16, kind="Internal")
    kdr = nc.dram_tensor("kdr", [NH, P, S], BF16, kind="Internal")
    vdr = nc.dram_tensor("vdr", [S, D], BF16, kind="Internal")
    ln1st = nc.dram_tensor("ln1st", [OWN, D], BF16, kind="Internal")
    x1dr = nc.dram_tensor("x1dr", [OWN, D], F32, kind="Internal")

    ctx_stack = ExitStack()
    with ctx_stack as stack:
        t = Ctx(nc, stack)

        SBUF_BASE = 16512

        def at(name, shape, dtype, off):
            return nc.alloc_sbuf_tensor_at(name, shape, dtype, offset=SBUF_BASE + off)

        pbq = [stack.enter_context(nc.psum_tensor(f"pbq{i}", [P, 1024], F32))
               for i in range(2)]
        pb = [pbq[0][:, 0:512], pbq[0][:, 512:1024],
              pbq[1][:, 0:512], pbq[1][:, 512:1024]] + \
             [stack.enter_context(nc.psum_tensor(f"pb{i}", [P, 512], F32))[:]
              for i in range(4, 8)]
        diagm = at("diagm", [P, 4, 512], BF16, 0)
        btab_sb = at("btab_sb", [P, 64], F32, 4096)
        onec = at("onec", [P, 1], BF16, 212832)
        oner = at("oner", [1, P], BF16, 208832)
        zr = at("zr", [1, 512], F32, 205760)
        zrb = at("zrb", [1, 512], BF16, 207808)
        kidx = at("kidx", [P, 10], I32, 4352)
        gb = at("gb", [P, 1], F32, 4416)
        gwt_sb = at("gwt_sb", [P, HD], BF16, 4448)
        gwsg_sb = at("gwsg_sb", [P, ND, 17], BF16, 4704)
        stat = at("stat", [P, 16], F32, 5280)
        cw = at("cw", [P, NT, 16], F32, 5344)
        sg = at("sg", [P, NT], F32, 5856)
        rsb = at("rsb", [P, 17], F32, 5888)
        r16 = at("r16", [P, 16], F32, 5984)
        mx8 = at("mx8", [P, 8], F32, 6048)
        xst = at("xst", [P, 2, D], F32, 6080)
        lntm = at("lntm", [P, 2, D], BF16, 22464)
        sqs = at("sqs", [P, D], F32, 30656)

        bst = at("bst", [P, 7, NT], F32, 209600)
        SSQ, S1, NEGM, M2, VAR, LNV, RSTD = range(7)

        def ln_stats(src_ap, src_regs, tt):
            t.emit("scalar",
                   lambda e, s=src_ap, tt=tt: e.activation(
                       sqs[:], s, AF.Square, accum_out=bst[:, SSQ, tt:tt + 1]),
                   reads=list(src_regs), writes=["sqs", ("bst", tt)])
            t.emit("vector",
                   lambda e, s=src_ap, tt=tt: e.reduce_sum(
                       bst[:, S1, tt:tt + 1], s, axis=AX.X),
                   reads=list(src_regs), writes=[("bst", tt)])

        def ln_chain():
            rd = [("bst", i) for i in range(NT)]
            t.emit("vector",
                   lambda e: e.tensor_scalar_mul(bst[:, NEGM, :], bst[:, S1, :], -1.0 / D),
                   reads=rd, writes=["bch"] + rd)
            t.emit("vector",
                   lambda e: e.tensor_mul(bst[:, M2, :], bst[:, NEGM, :], bst[:, NEGM, :]),
                   reads=["bch"], writes=["bch"])
            t.emit("vector",
                   lambda e: e.scalar_tensor_tensor(
                       out=bst[:, VAR, :], in0=bst[:, SSQ, :], scalar=1.0 / D,
                       in1=bst[:, M2, :], op0=ALU.mult, op1=ALU.subtract),
                   reads=["bch"], writes=["bch"])
            t.emit("vector",
                   lambda e: e.tensor_scalar_add(bst[:, VAR, :], bst[:, VAR, :], EPS),
                   reads=["bch"], writes=["bch"])
            t.emit("scalar",
                   lambda e: e.activation(bst[:, LNV, :], bst[:, VAR, :], AF.Ln),
                   reads=["bch"], writes=["bch"])
            t.emit("scalar",
                   lambda e: e.activation(bst[:, RSTD, :], bst[:, LNV, :], AF.Exp, scale=-0.5),
                   reads=["bch"], writes=["bch"])

        def ln_apply(src_ap, src_regs, out_ap, out_regs, tt):
            t.emit("vector",
                   lambda e, s=src_ap, o=out_ap, tt=tt: e.tensor_scalar(
                       out=o, in0=s, scalar1=bst[:, NEGM, tt:tt + 1],
                       scalar2=bst[:, RSTD, tt:tt + 1], op0=ALU.add, op1=ALU.mult),
                   reads=list(src_regs) + ["bch"], writes=list(out_regs))

        def stcol(par, i):
            return stat[:, par * 8 + i: par * 8 + i + 1]

        def ln_tile(src_ap, src_regs, out_ap, out_regs, par, extra_reads=()):
            sreg = ("stat", par)
            qreg = "sqs"
            t.emit("scalar",
                   lambda e, s=src_ap, a=stcol(par, SSQ):
                       e.activation(sqs[:], s, AF.Square, accum_out=a),
                   reads=list(src_regs) + list(extra_reads),
                   writes=[qreg, sreg])
            t.emit("vector",
                   lambda e, s=src_ap, o=stcol(par, S1): e.reduce_sum(o, s, axis=AX.X),
                   reads=list(src_regs), writes=[sreg])
            t.emit("vector",
                   lambda e: e.tensor_scalar_mul(stcol(par, NEGM), stcol(par, S1), -1.0 / D),
                   reads=[sreg], writes=[sreg])
            t.emit("vector",
                   lambda e: e.tensor_mul(stcol(par, M2), stcol(par, NEGM), stcol(par, NEGM)),
                   reads=[sreg], writes=[sreg])
            t.emit("vector",
                   lambda e: e.scalar_tensor_tensor(
                       out=stcol(par, VAR), in0=stcol(par, SSQ), scalar=1.0 / D,
                       in1=stcol(par, M2), op0=ALU.mult, op1=ALU.subtract),
                   reads=[sreg], writes=[sreg])
            t.emit("vector",
                   lambda e: e.tensor_scalar_add(stcol(par, VAR), stcol(par, VAR), EPS),
                   reads=[sreg], writes=[sreg])
            t.emit("scalar",
                   lambda e: e.activation(stcol(par, LNV), stcol(par, VAR), AF.Ln),
                   reads=[sreg], writes=[sreg])
            t.emit("scalar",
                   lambda e: e.activation(stcol(par, RSTD), stcol(par, LNV), AF.Exp, scale=-0.5),
                   reads=[sreg], writes=[sreg])
            t.emit("vector",
                   lambda e, s=src_ap, o=out_ap: e.tensor_scalar(
                       out=o, in0=s, scalar1=stcol(par, NEGM), scalar2=stcol(par, RSTD),
                       op0=ALU.add, op1=ALU.mult),
                   reads=list(src_regs) + [sreg], writes=list(out_regs))

        for r in range(4):
            t.emit("gpsimd", lambda e, r=r: e.memset(diagm[:, r, :], 1.0),
                   writes=[("dm", r)])
            t.emit("gpsimd",
                   lambda e, r=r: e.affine_select(
                       out=diagm[:, r, :], in_=diagm[:, r, :],
                       compare_op=ALU.is_ge, fill=0.0, base=-128 * r,
                       pattern=[[1, 512]], channel_multiplier=-1),
                   reads=[("dm", r)], writes=[("dm", r)])
        t.emit("sync", lambda e: e.dma_start(btab_sb[:], btab[:]), writes=["btab"], dma=True)
        t.emit("gpsimd", lambda e: e.memset(onec[:], 1.0), writes=["onec"])
        t.emit("gpsimd", lambda e: e.memset(oner[:], 1.0), writes=["oner"])
        t.emit("sync", lambda e: e.dma_start(kidx[:], kvidx[:]), writes=["kidx"], dma=True)
        t.emit("sync", lambda e: e.dma_start(gb[:], gateb[:]), writes=["gb"], dma=True)
        t.emit("gpsimd", lambda e: e.dma_start(gwt_sb[:], gwt[:]), writes=["gwt"], dma=True)
        t.emit("gpsimd",
               lambda e: e.dma_start(gwsg_sb[:], gwsg.rearrange("(c p) e -> p c e", p=P)),
               writes=["gwsg"], dma=True)


        lnfb = [at("lnfb0", [P, ND, 1024], BF16, 38848),
                at("lnfb1", [P, ND, 1024], BF16, 71616)]
        kwres = at("kwres", [P, ND, NH, HD], BF16, 104384)
        kev = at("kev", [P, 2, 1024], BF16, 169920)
        vev = at("vev", [P, 2, D], BF16, 174016)

        t.emit("gpsimd",
               lambda e: e.dma_start(
                   kwres[:], kwt.rearrange("(c p) (h d) -> p c h d", p=P, d=HD)),
               writes=["kwres"], dma=True)
        for R in range(4):
            rp = R % 2
            for tt in range(8):
                g = R * 8 + tt
                par = tt % 2
                t.emit("sync",
                       lambda e, g=g, par=par: e.dma_start(
                           xst[:, par, :], xfull[g * P:(g + 1) * P, :]),
                       writes=[("xst", par)], dma=True)
                ln_stats(xst[:, par, :], [("xst", par)], tt)
            ln_chain()
            for tt in range(8):
                g = R * 8 + tt
                par = tt % 2
                t.emit("sync",
                       lambda e, g=g, par=par: e.dma_start(
                           xst[:, par, :], xfull[g * P:(g + 1) * P, :]),
                       writes=[("xst", par)], dma=True)
                ln_apply(xst[:, par, :], [("xst", par)],
                         lntm[:, par, :], [("lntm", par)], tt)
                t.emit("sync",
                       lambda e, g=g, par=par: e.dma_start(
                           ln0st[g * P:(g + 1) * P, :], lntm[:, par, :]),
                       reads=[("lntm", par)], writes=[("ln0st", R)], dma=True,
                       dsem=f"d_ln0st{par}")
            for c in range(ND):
                t.emit("scalar",
                       lambda e, R=R, c=c, rp=rp: e.dma_start(
                           lnfb[rp][:, c, :],
                           ln0st[R * 1024:(R + 1) * 1024, c * P:(c + 1) * P],
                           transpose=True),
                       reads=[("ln0st", R)], writes=[("lnfb", rp)], dma=True)
            t.emit("sync",
                   lambda e, R=R, rp=rp: e.dma_start(
                       lnfm.rearrange("c p t -> p c t")[:, :, R * 1024:(R + 1) * 1024],
                       lnfb[rp][:]),
                   reads=[("lnfb", rp)], writes=[("lnfm", R)], dma=True,
                   dsem=f"d_lnfm{rp}")
            for h in range(NH):
                bb = ((R * NH + h) % 2) * 2
                for c in range(ND):
                    for g in range(2):
                        t.emit("tensor",
                               lambda e, c=c, g=g, h=h, bb=bb, rp=rp: e.matmul(
                                   pb[bb + g][:, 0:512],
                                   kwres[:, c, h, :],
                                   lnfb[rp][:, c, g * 512:(g + 1) * 512],
                                   start=(c == 0), stop=(c == ND - 1)),
                               reads=["kwres", ("lnfb", rp)],
                               writes=[f"pb{bb + g}"],
                               pe_inc=(c == ND - 1))
                kp = h % 2
                for g in range(2):
                    t.emit("scalar",
                           lambda e, g=g, bb=bb, kp=kp: e.activation(
                               kev[:, kp, g * 512:(g + 1) * 512], pb[bb + g][:, 0:512],
                               AF.Copy, scale=QSCALE),
                           reads=[f"pb{bb + g}"], writes=[("kev", kp)])
                t.emit("sync",
                       lambda e, h=h, R=R, kp=kp: e.dma_start(
                           kdr[h, :, R * 1024:(R + 1) * 1024], kev[:, kp, :]),
                       reads=[("kev", kp)], writes=[("kdr", h)], dma=True,
                       dsem=f"d_kdr{kp}")
        t.barrier()

        vw_ap = kwres[:].rearrange("p c h d -> p c (h d)")
        t.emit("gpsimd",
               lambda e: e.dma_start(
                   vw_ap, vwt.rearrange("(c p) e -> p c e", p=P)),
               writes=["kwres"], dma=True)
        for R in range(4):
            rp = R % 2
            t.emit("sync",
                   lambda e, R=R, rp=rp: e.dma_start(
                       lnfb[rp][:],
                       lnfm.rearrange("c p t -> p c t")[:, :, R * 1024:(R + 1) * 1024]),
                   reads=[("lnfm", R)], writes=[("lnfb", rp)], dma=True)
            for tt in range(8):
                T = R * 8 + tt
                bb = (T % 2) * 4
                vp = T % 2
                for c in range(ND):
                    for eg in range(4):
                        t.emit("tensor",
                               lambda e, c=c, eg=eg, tt=tt, bb=bb, rp=rp: e.matmul(
                                   pb[bb + eg][:, 0:512],
                                   lnfb[rp][:, c, tt * P:(tt + 1) * P],
                                   vw_ap[:, c, eg * 512:(eg + 1) * 512],
                                   start=(c == 0), stop=(c == ND - 1)),
                               reads=["kwres", ("lnfb", rp)],
                               writes=[f"pb{bb + eg}"],
                               pe_inc=(c == ND - 1))
                for eg in range(4):
                    t.emit("scalar",
                           lambda e, eg=eg, bb=bb, vp=vp: e.activation(
                               vev[:, vp, eg * 512:(eg + 1) * 512],
                               pb[bb + eg][:, 0:512], AF.Copy),
                           reads=[f"pb{bb + eg}"], writes=[("vev", vp)])
                t.emit("sync",
                       lambda e, T=T, vp=vp: e.dma_start(
                           vdr[T * P:(T + 1) * P, :], vev[:, vp, :]),
                       reads=[("vev", vp)], writes=["v0"], dma=True,
                       dsem=f"d_v0{vp}")
        t.barrier()

        q_fm = at("q_fm", [P, NH, 1024], BF16, 38848)
        gate_all = at("gate_all", [P, NH, 1024], BF16, 71616)
        ownfm = at("ownfm", [P, ND, 1024], BF16, 104384)
        qwres = at("qwres", [P, ND, 8, HD], BF16, 137152)

        for tt in range(NT):
            par = tt % 2
            t.emit("sync",
                   lambda e, tt=tt, par=par: e.dma_start(
                       xst[:, par, :], xown[tt * P:(tt + 1) * P, :]),
                   writes=[("xst", par)], dma=True)
            ln_stats(xst[:, par, :], [("xst", par)], tt)
        ln_chain()
        for tt in range(NT):
            par = tt % 2
            t.emit("sync",
                   lambda e, tt=tt, par=par: e.dma_start(
                       xst[:, par, :], xown[tt * P:(tt + 1) * P, :]),
                   writes=[("xst", par)], dma=True)
            ln_apply(xst[:, par, :], [("xst", par)],
                     lntm[:, par, :], [("lntm", par)], tt)
            t.emit("sync",
                   lambda e, tt=tt, par=par: e.dma_start(
                       ownst[tt * P:(tt + 1) * P, :], lntm[:, par, :]),
                   reads=[("lntm", par)], writes=["ownst"], dma=True,
                   dsem=f"d_ownst{par}")
        for c in range(ND):
            t.emit("scalar",
                   lambda e, c=c: e.dma_start(
                       ownfm[:, c, :], ownst[:, c * P:(c + 1) * P], transpose=True),
                   reads=["ownst"], writes=["ownfm"], dma=True)
        for half in range(2):
            t.emit("gpsimd",
                   lambda e, half=half: e.dma_start(
                       qwres[:],
                       qwt.rearrange("(c p) (h d) -> p c h d", p=P, d=HD)[:, :, half * 8:(half + 1) * 8, :]),
                   writes=["qwres"], dma=True)
            for hh in range(8):
                h = half * 8 + hh
                bb = (h % 2) * 2
                for c in range(ND):
                    for g in range(2):
                        t.emit("tensor",
                               lambda e, c=c, g=g, hh=hh, bb=bb: e.matmul(
                                   pb[bb + g][:, 0:512],
                                   qwres[:, c, hh, :],
                                   ownfm[:, c, g * 512:(g + 1) * 512],
                                   start=(c == 0), stop=(c == ND - 1)),
                               reads=["qwres", "ownfm"],
                               writes=[f"pb{bb + g}"],
                               pe_inc=(c == ND - 1))
                for g in range(2):
                    t.emit("scalar",
                           lambda e, h=h, g=g, bb=bb: e.activation(
                               q_fm[:, h, g * 512:(g + 1) * 512],
                               pb[bb + g][:, 0:512], AF.Copy),
                           reads=[f"pb{bb + g}"], writes=["qfm"])
        for h in range(NH):
            bb = (h % 2) * 2
            for g in range(2):
                t.emit("tensor",
                       lambda e, h=h, g=g, bb=bb: e.matmul(
                           pb[bb + g][:, 0:512], gwt_sb[:],
                           q_fm[:, h, g * 512:(g + 1) * 512],
                           start=True, stop=True),
                       reads=["gwt", "qfm"], writes=[f"pb{bb + g}"], pe_inc=True)
                t.emit("scalar",
                       lambda e, h=h, g=g, bb=bb: e.activation(
                           gate_all[:, h, g * 512:(g + 1) * 512],
                           pb[bb + g][:, 0:512], AF.Sigmoid, bias=gb[:]),
                       reads=[f"pb{bb + g}", "gb"], writes=["gate"])
        t.barrier()

        gav = at("gav", [P, NH, 1024], BF16, 104384)
        kbulk = [at("kbulk0", [P, NKB * P], BF16, 137152),
                 at("kbulk1", [P, NKB * P], BF16, 144320)]
        vbulk = [at("vbulk0", [P, NKB, P], BF16, 151488),
                 at("vbulk1", [P, NKB, P], BF16, 158656)]
        kdg = [at("kdg0", [P, 512], BF16, 165824),
               at("kdg1", [P, 512], BF16, 166848)]
        vdg = [at("vdg0", [P, 4, P], BF16, 167872),
               at("vdg1", [P, 4, P], BF16, 168896)]
        exptb = at("exptb", [P, 4, 512], BF16, 169920)
        avsb = [at("avsb0", [P, 512], BF16, 174016),
                at("avsb1", [P, 512], BF16, 175040)]

        kdr_blocks = kdr.rearrange("h p (b t) -> (h p b) t", t=P)
        job = 0
        for g in range(2):
            n = NKA if g == 0 else NKB
            boff = 0 if g == 0 else NKA
            boff2 = 41 if g == 0 else 41 + NKA // 2
            for h in range(NH):
                jp = job % 2
                avb = 4 + (job % 2)
                zb = 6 + (job % 2)
                t.emit("sync",
                       lambda e, h=h, n=n, jp=jp: e.dma_start(
                           kbulk[jp][:, 0:n * P], kdr[h, :, 0:n * P]),
                       reads=[("kdr", h)], writes=[("kbulk", jp)], dma=True)
                t.emit("sync",
                       lambda e, h=h, n=n, jp=jp: e.dma_start(
                           vbulk[jp][:, 0:n, :],
                           vdr.rearrange("(b p) e -> p b e", p=P)[:, 0:n, h * HD:(h + 1) * HD]),
                       reads=["v0"], writes=[("vbulk", jp)], dma=True)
                t.emit("gpsimd",
                       lambda e, h=h, g=g, jp=jp: e.indirect_dma_start(
                           kdg[jp][:], None, kdr_blocks[:, :],
                           IndirectOffsetOnAxis(kidx[:, g:g + 1], 0),
                           element_offset=h * P * S),
                       reads=[("kdr", h), "kidx"], writes=[("kdg", jp)], dma=True)
                for i in range(4):
                    t.emit("gpsimd",
                           lambda e, h=h, g=g, i=i, jp=jp: e.indirect_dma_start(
                               vdg[jp][:, i, :], None, vdr[:, :],
                               IndirectOffsetOnAxis(kidx[:, 2 + g * 4 + i: 3 + g * 4 + i], 0),
                               element_offset=h * HD),
                           reads=["v0", "kidx"], writes=[("vdg", jp)], dma=True)
                qslice = q_fm[:, h, g * 512:(g + 1) * 512]

                def sc_stage(s):
                    sp = s % 4
                    diag = s >= n
                    i = s - n
                    if not diag:
                        klhs = (lambda e, s=s, jp=jp: kbulk[jp][:, s * P:(s + 1) * P])
                        kreads = [("kbulk", jp)]
                    else:
                        klhs = (lambda e, i=i, jp=jp: kdg[jp][:, i * P:(i + 1) * P])
                        kreads = [("kdg", jp)]
                    t.emit("tensor",
                           lambda e, klhs=klhs, qs=qslice, sp=sp: e.matmul(
                               pb[sp][:, 0:512], klhs(e), qs,
                               start=True, stop=True),
                           reads=kreads + ["qfm"], writes=[f"pb{sp}"],
                           pe_inc=True)
                    if s % 2 == 0:
                        return
                    q2 = sp // 2
                    bcol = 40 if diag else boff2 + s // 2
                    t.emit("scalar",
                           lambda e, q2=q2, sp=sp, bcol=bcol: e.activation(
                               exptb[:, sp - 1:sp + 1, :], pbq[q2][:, 0:1024],
                               AF.Exp, bias=btab_sb[:, bcol:bcol + 1]),
                           reads=[f"pb{sp - 1}", f"pb{sp}", "btab"],
                           writes=[("expt", sp - 1), ("expt", sp)])
                    for s2 in (s - 1, s):
                        if s2 >= n:
                            i2 = s2 - n
                            t.emit("vector",
                                   lambda e, sp2=s2 % 4, i2=i2: e.tensor_mul(
                                       exptb[:, sp2, :], exptb[:, sp2, :],
                                       diagm[:, i2, :]),
                                   reads=[("expt", s2 % 4), ("dm", i2)],
                                   writes=[("expt", s2 % 4)])

                def av_stage(s):
                    sp = s % 4
                    diag = s >= n
                    i = s - n
                    if not diag:
                        vlhs = (lambda e, s=s, jp=jp: vbulk[jp][:, s, :])
                        vreads = [("vbulk", jp)]
                    else:
                        vlhs = (lambda e, i=i, jp=jp: vdg[jp][:, i, :])
                        vreads = [("vdg", jp)]
                    t.emit("tensor",
                           lambda e, vlhs=vlhs, sp=sp, avb=avb, s=s, last=(s == n + 3): e.matmul(
                               pb[avb][:, 0:512], vlhs(e), exptb[:, sp, :],
                               start=(s == 0), stop=last),
                           reads=vreads + [("expt", sp)], writes=[f"pb{avb}"],
                           pe_inc=(s == n + 3))
                    t.emit("tensor",
                           lambda e, sp=sp, zb=zb, s=s, last=(s == n + 3): e.matmul(
                               pb[zb][0:1, 0:512], onec[:], exptb[:, sp, :],
                               start=(s == 0), stop=last),
                           reads=["onec", ("expt", sp)], writes=[f"pb{zb}"],
                           pe_inc=(s == n + 3))

                LAG = 2
                for s in range(n + 4):
                    sc_stage(s)
                    if s >= LAG:
                        av_stage(s - LAG)
                for s in range(n + 4 - LAG, n + 4):
                    av_stage(s)

                t.emit("scalar",
                       lambda e, zb=zb: e.activation(zr[:], pb[zb][0:1, 0:512], AF.Ln),
                       reads=[f"pb{zb}"], writes=["zr"])
                t.emit("scalar",
                       lambda e: e.activation(zrb[:], zr[:], AF.Exp, scale=-1.0),
                       reads=["zr"], writes=["zrb"])
                bbc = (n + 4) % 4
                t.emit("tensor",
                       lambda e, bbc=bbc: e.matmul(
                           pb[bbc][:, 0:512], oner[:], zrb[:],
                           start=True, stop=True),
                       reads=["oner", "zrb"], writes=[f"pb{bbc}"], pe_inc=True)
                t.emit("scalar",
                       lambda e, avb=avb, jp=jp: e.activation(
                           avsb[jp][:], pb[avb][:, 0:512], AF.Copy),
                       reads=[f"pb{avb}"], writes=[("avsb", jp)])
                t.emit("vector",
                       lambda e, h=h, g=g, jp=jp: e.tensor_mul(
                           avsb[jp][:], avsb[jp][:],
                           gate_all[:, h, g * 512:(g + 1) * 512]),
                       reads=[("avsb", jp), "gate"], writes=[("avsb", jp)])
                t.emit("vector",
                       lambda e, h=h, g=g, jp=jp, bbc=bbc: e.tensor_mul(
                           gav[:, h, g * 512:(g + 1) * 512], avsb[jp][:],
                           pb[bbc][:, 0:512]),
                       reads=[("avsb", jp), f"pb{bbc}"], writes=["gav"])
                job += 1
        t.barrier()

        lnf1 = at("lnf1", [P, ND, 1024], BF16, 38848)
        owres = at("owres", [P, ND, D], BF16, 137152)
        aosb = at("aosb", [P, D], F32, 71616)
        ansb = at("ansb", [P, D], F32, 79808)

        t.emit("gpsimd",
               lambda e: e.dma_start(owres[:], owt.rearrange("(c p) e -> p c e", p=P)),
               writes=["owres"], dma=True)
        for tt in range(NT):
            bb = (tt % 2) * 4
            par = tt % 2
            for c in range(ND):
                for eg in range(4):
                    t.emit("tensor",
                           lambda e, c=c, eg=eg, tt=tt, bb=bb: e.matmul(
                               pb[bb + eg][:, 0:512],
                               gav[:, c, tt * P:(tt + 1) * P],
                               owres[:, c, eg * 512:(eg + 1) * 512],
                               start=(c == 0), stop=(c == ND - 1)),
                           reads=["gav", "owres"], writes=[f"pb{bb + eg}"],
                           pe_inc=(c == ND - 1))
            for eg in range(4):
                t.emit("scalar",
                       lambda e, eg=eg, bb=bb: e.activation(
                           aosb[:, eg * 512:(eg + 1) * 512], pb[bb + eg][:, 0:512],
                           AF.Copy),
                       reads=[f"pb{bb + eg}"], writes=["aosb"])
            ln_tile(aosb[:], ["aosb"], ansb[:], ["ansb"], par)
            t.emit("sync",
                   lambda e, tt=tt, par=par: e.dma_start(
                       xst[:, par, :], xown[tt * P:(tt + 1) * P, :]),
                   writes=[("xst", par)], dma=True)
            t.emit("vector",
                   lambda e, par=par: e.tensor_add(ansb[:], ansb[:], xst[:, par, :]),
                   reads=["ansb", ("xst", par)], writes=["ansb"])
            t.emit("sync",
                   lambda e, tt=tt: e.dma_start(x1dr[tt * P:(tt + 1) * P, :], ansb[:]),
                   reads=["ansb"], writes=["x1dr"], dma=True)
            ln_tile(ansb[:], ["ansb"], lntm[:, par, :], [("lntm", par)], par)
            t.emit("sync",
                   lambda e, tt=tt, par=par: e.dma_start(
                       ln1st[tt * P:(tt + 1) * P, :], lntm[:, par, :]),
                   reads=[("lntm", par)], writes=["ln1st"], dma=True,
                   dsem=f"d_ln1st{par}")
        for c in range(ND):
            t.emit("scalar",
                   lambda e, c=c: e.dma_start(
                       lnf1[:, c, :], ln1st[:, c * P:(c + 1) * P], transpose=True),
                   reads=["ln1st"], writes=["lnf1"], dma=True)
        t.barrier()

        for tt in range(NT):
            bk = tt % 2
            for c in range(ND):
                t.emit("tensor",
                       lambda e, c=c, tt=tt, bk=bk: e.matmul(
                           pb[bk][:, 0:17],
                           lnf1[:, c, tt * P:(tt + 1) * P], gwsg_sb[:, c, :],
                           start=(c == 0), stop=(c == ND - 1)),
                       reads=["lnf1", "gwsg"], writes=[f"pb{bk}"],
                       pe_inc=(c == ND - 1))
            t.emit("scalar",
                   lambda e, bk=bk: e.activation(rsb[:], pb[bk][:, 0:17], AF.Copy),
                   reads=[f"pb{bk}"], writes=["rsb"])
            t.emit("vector", lambda e: e.reduce_max(stat[:, 0:1], rsb[:, 0:16], axis=AX.X),
                   reads=["rsb"], writes=[("stat", 0)])
            t.emit("vector", lambda e: e.tensor_scalar_mul(stat[:, 1:2], stat[:, 0:1], -1.0),
                   reads=[("stat", 0)], writes=[("stat", 0)])
            t.emit("scalar",
                   lambda e: e.activation(r16[:], rsb[:, 0:16], AF.Exp,
                                          bias=stat[:, 1:2], accum_out=stat[:, 2:3]),
                   reads=["rsb", ("stat", 0)], writes=["r16", ("stat", 0)])
            t.emit("vector", lambda e: e.reciprocal(stat[:, 3:4], stat[:, 2:3]),
                   reads=[("stat", 0)], writes=[("stat", 0)])
            t.emit("vector",
                   lambda e: e.tensor_scalar_mul(r16[:], r16[:], stat[:, 3:4]),
                   reads=["r16", ("stat", 0)], writes=["r16"])
            t.emit("vector", lambda e: e.max(out=mx8[:], in_=r16[:]),
                   reads=["r16"], writes=["mx8"])
            t.emit("vector", lambda e: e.memset(mx8[:, 4:8], 0.0),
                   reads=["mx8"], writes=["mx8"])
            t.emit("vector",
                   lambda e: e.match_replace(out=rsb[:, 0:16], in_to_replace=mx8[:],
                                             in_values=r16[:], imm_value=0.0),
                   reads=["mx8", "r16"], writes=["rsb"])
            t.emit("vector",
                   lambda e, tt=tt: e.tensor_sub(cw[:, tt, :], r16[:], rsb[:, 0:16]),
                   reads=["r16", "rsb"], writes=["cw"])
            t.emit("scalar",
                   lambda e: e.activation(stat[:, 4:5], rsb[:, 16:17], AF.Exp, scale=-1.0),
                   reads=["rsb"], writes=[("stat", 0)])
            t.emit("vector",
                   lambda e: e.tensor_scalar_add(stat[:, 4:5], stat[:, 4:5], 1.0),
                   reads=[("stat", 0)], writes=[("stat", 0)])
            t.emit("vector",
                   lambda e, tt=tt: e.reciprocal(sg[:, tt:tt + 1], stat[:, 4:5]),
                   reads=[("stat", 0)], writes=["sg"])
        t.barrier()

        acc = at("acc", [P, NT, D], F32, 71616)
        he = at("he", [P, NIE, 1024], BF16, 137152)
        w1r = [at("w1r0", [P, ND, P], BF16, 159680),
               at("w1r1", [P, ND, P], BF16, 163776)]
        w3r = [at("w3r0", [P, ND, P], BF16, 167872),
               at("w3r1", [P, ND, P], BF16, 171968)]
        w2p = [at("w2p0", [P, NIE, 512], BF16, 176064),
               at("w2p1", [P, NIE, 512], BF16, 187328)]
        ssc = at("ssc", [P, 2, 512], F32, 198592)

        def expert(widx, first):
            kind, ix = widx
            if kind == "s":
                w1src = sw1.rearrange("(c p) i -> p c i", p=P)
                w3src = sw3.rearrange("(c p) i -> p c i", p=P)
                w2src = sw2.rearrange("(c p) e -> p c e", p=P)
                icoff = ix * NIE
            else:
                w1src = w1[ix * D:(ix + 1) * D, :].rearrange("(c p) i -> p c i", p=P)
                w3src = w3[ix * D:(ix + 1) * D, :].rearrange("(c p) i -> p c i", p=P)
                w2src = w2[ix * 1408:(ix + 1) * 1408, :].rearrange("(c p) e -> p c e", p=P)
                icoff = 0
            for ic in range(NIE):
                wp = ic % 2
                t.emit("gpsimd",
                       lambda e, ic=ic, wp=wp, w1src=w1src: e.dma_start(
                           w1r[wp][:],
                           w1src[:, :, (icoff + ic) * P:(icoff + ic + 1) * P]),
                       writes=[("w1r", wp)], dma=True)
                t.emit("gpsimd",
                       lambda e, ic=ic, wp=wp, w3src=w3src: e.dma_start(
                           w3r[wp][:],
                           w3src[:, :, (icoff + ic) * P:(icoff + ic + 1) * P]),
                       writes=[("w3r", wp)], dma=True)
                bb = (ic % 2) * 4
                for c in range(ND):
                    for g in range(2):
                        t.emit("tensor",
                               lambda e, c=c, g=g, wp=wp, bb=bb: e.matmul(
                                   pb[bb + g][:, 0:512], w1r[wp][:, c, :],
                                   lnf1[:, c, g * 512:(g + 1) * 512],
                                   start=(c == 0), stop=(c == ND - 1)),
                               reads=[("w1r", wp), "lnf1"], writes=[f"pb{bb + g}"],
                               pe_inc=(c == ND - 1))
                for c in range(ND):
                    for g in range(2):
                        t.emit("tensor",
                               lambda e, c=c, g=g, wp=wp, bb=bb: e.matmul(
                                   pb[bb + 2 + g][:, 0:512], w3r[wp][:, c, :],
                                   lnf1[:, c, g * 512:(g + 1) * 512],
                                   start=(c == 0), stop=(c == ND - 1)),
                               reads=[("w3r", wp), "lnf1"], writes=[f"pb{bb + 2 + g}"],
                               pe_inc=(c == ND - 1))
                for g in range(2):
                    t.emit("scalar",
                           lambda e, g=g, bb=bb: e.activation(
                               ssc[:, g, :], pb[bb + g][:, 0:512], AF.Silu),
                           reads=[f"pb{bb + g}"], writes=[("ssc", g)])
                    t.emit("vector",
                           lambda e, g=g, ic=ic, bb=bb: e.tensor_mul(
                               he[:, ic, g * 512:(g + 1) * 512], ssc[:, g, :],
                               pb[bb + 2 + g][:, 0:512]),
                           reads=[("ssc", g), f"pb{bb + 2 + g}"],
                           writes=[("he", ic)])
            for eg in range(4):
                wp = eg % 2
                t.emit("gpsimd",
                       lambda e, eg=eg, wp=wp, w2src=w2src: e.dma_start(
                           w2p[wp][:],
                           w2src[:, icoff:icoff + NIE, eg * 512:(eg + 1) * 512]),
                       writes=[("w2p", wp)], dma=True)
                for tp in range(4):
                    bb = (tp % 2) * 2
                    for ic in range(NIE):
                        for i in range(2):
                            t.emit("tensor",
                                   lambda e, ic=ic, i=i, tp=tp, wp=wp, bb=bb: e.matmul(
                                       pb[bb + i][:, 0:512],
                                       he[:, ic, (tp * 2 + i) * P:(tp * 2 + i + 1) * P],
                                       w2p[wp][:, ic, :],
                                       start=(ic == 0), stop=(ic == NIE - 1)),
                                   reads=[("he", ic), ("w2p", wp)],
                                   writes=[f"pb{bb + i}"],
                                   pe_inc=(ic == NIE - 1))
                    for i in range(2):
                        tt = tp * 2 + i
                        scal = (lambda e, tt=tt: sg[:, tt:tt + 1]) if kind == "s" \
                            else (lambda e, tt=tt, ix=ix: cw[:, tt, ix:ix + 1])
                        sreads = ["sg"] if kind == "s" else ["cw"]
                        if first:
                            t.emit("vector",
                                   lambda e, tt=tt, eg=eg, i=i, bb=bb, scal=scal:
                                       e.tensor_scalar_mul(
                                           acc[:, tt, eg * 512:(eg + 1) * 512],
                                           pb[bb + i][:, 0:512], scal(e)),
                                   reads=[f"pb{bb + i}"] + sreads,
                                   writes=[("acc", tt)])
                        else:
                            t.emit("vector",
                                   lambda e, tt=tt, eg=eg, i=i, bb=bb, scal=scal:
                                       e.scalar_tensor_tensor(
                                           out=acc[:, tt, eg * 512:(eg + 1) * 512],
                                           in0=pb[bb + i][:, 0:512], scalar=scal(e),
                                           in1=acc[:, tt, eg * 512:(eg + 1) * 512],
                                           op0=ALU.mult, op1=ALU.add),
                                   reads=[f"pb{bb + i}", ("acc", tt)] + sreads,
                                   writes=[("acc", tt)])

        for q in range(4):
            expert(("s", q), first=(q == 0))
        for e_ in range(NE):
            expert(("e", e_), first=False)
        t.barrier()

        jns = at("jns", [P, 2, D], F32, 137152)
        for tt in range(NT):
            ln_stats(acc[:, tt, :], [("acc", tt)], tt)
        ln_chain()
        for tt in range(NT):
            par = tt % 2
            ln_apply(acc[:, tt, :], [("acc", tt)], jns[:, par, :], [("jns", par)], tt)
            t.emit("sync",
                   lambda e, tt=tt, par=par: e.dma_start(
                       xst[:, par, :], x1dr[tt * P:(tt + 1) * P, :]),
                   reads=["x1dr"], writes=[("xst", par)], dma=True)
            t.emit("vector",
                   lambda e, par=par: e.tensor_add(
                       jns[:, par, :], jns[:, par, :], xst[:, par, :]),
                   reads=[("jns", par), ("xst", par)], writes=[("jns", par)])
            t.emit("sync",
                   lambda e, tt=tt, par=par: e.dma_start(
                       out[tt * P:(tt + 1) * P, :], jns[:, par, :]),
                   reads=[("jns", par)], writes=["outw"], dma=True,
                   dsem=f"d_out{par}")
        t.final_waits()

        block = stack.enter_context(nc.Block())
        t.replay(block)
    return nc


def kernel(L0_attn_x, L0_attn_q_w, L0_attn_k_w, L0_attn_v_w, L0_attn_o_w,
           L0_attn_gate_w, L0_attn_gate_b, L0_ffn_gate_w, L0_ffn_w1, L0_ffn_w3,
           L0_ffn_w2, L0_ffn_s_w1, L0_ffn_s_w3, L0_ffn_s_w2, L0_ffn_sg_w):
    x = np.ascontiguousarray(np.asarray(L0_attn_x, dtype=np.float32))
    B = x.shape[0]
    shared = {
        "qwt": np.ascontiguousarray(np.asarray(L0_attn_q_w, np.float32).T),
        "kwt": np.ascontiguousarray(np.asarray(L0_attn_k_w, np.float32).T),
        "vwt": np.ascontiguousarray(np.asarray(L0_attn_v_w, np.float32).T),
        "owt": np.ascontiguousarray(np.asarray(L0_attn_o_w, np.float32).T),
        "gwt": np.ascontiguousarray(np.asarray(L0_attn_gate_w, np.float32).T),
        "gateb": np.ascontiguousarray(np.asarray(L0_attn_gate_b, np.float32).reshape(P, 1)),
        "gwsg": np.ascontiguousarray(np.concatenate(
            [np.asarray(L0_ffn_gate_w, np.float32),
             np.asarray(L0_ffn_sg_w, np.float32)], axis=1)),
        "sw1": np.ascontiguousarray(np.asarray(L0_ffn_s_w1, np.float32)),
        "sw3": np.ascontiguousarray(np.asarray(L0_ffn_s_w3, np.float32)),
        "sw2": np.ascontiguousarray(np.asarray(L0_ffn_s_w2, np.float32)),
        "w1": np.ascontiguousarray(np.asarray(L0_ffn_w1, np.float32).reshape(NE * D, 1408)),
        "w3": np.ascontiguousarray(np.asarray(L0_ffn_w3, np.float32).reshape(NE * D, 1408)),
        "w2": np.ascontiguousarray(np.asarray(L0_ffn_w2, np.float32).reshape(NE * 1408, D)),
    }

    in_maps = []
    for c in range(8):
        b, j = c // 4, c % 4
        a, bb = j, 7 - j
        xown = np.ascontiguousarray(np.concatenate(
            [x[b, 512 * a:512 * (a + 1)], x[b, 512 * bb:512 * (bb + 1)]], axis=0))
        btab = np.full((P, 64), -1e30, np.float32)
        for s in range(NKA):
            if s < 4 * a:
                btab[:, s] = 14.0
        for s in range(NKB):
            if s < 4 * bb:
                btab[:, NKA + s] = 14.0
        btab[:, 40] = 14.0
        for k in range(NKA // 2):
            btab[:, 41 + k] = 14.0 if 2 * k + 1 < 4 * a else -1e30
        for k in range(NKB // 2):
            btab[:, 41 + NKA // 2 + k] = 14.0 if 2 * k + 1 < 4 * bb else -1e30
        kvidx = np.zeros((P, 10), np.int32)
        pr = np.arange(P, dtype=np.int32)
        kvidx[:, 0] = pr * 32 + 4 * a
        kvidx[:, 1] = pr * 32 + 4 * bb
        for i in range(4):
            kvidx[:, 2 + i] = (4 * a + i) * P + pr
            kvidx[:, 6 + i] = (4 * bb + i) * P + pr
        m = {"xfull": x[b], "xown": xown,
             "btab": btab, "kvidx": kvidx}
        m.update(shared)
        in_maps.append(m)

    nc = build_program()
    res = run_bass_kernel_spmd(nc, in_maps, core_ids=list(range(8)))

    y = np.empty((B, S, D), np.float32)
    for c in range(8):
        b, j = c // 4, c % 4
        a, bb = j, 7 - j
        o = res.results[c]["out"]
        y[b, 512 * a:512 * (a + 1)] = o[0:512]
        y[b, 512 * bb:512 * (bb + 1)] = o[512:1024]
    return y


# version 14
# speedup vs baseline: 1.0900x; 1.0900x over previous
import numpy as np
import ml_dtypes
from contextlib import ExitStack

import concourse.bass as bass
import concourse.mybir as mybir
from concourse.bass import IndirectOffsetOnAxis
from concourse.bass_utils import run_bass_kernel_spmd

F32 = mybir.dt.float32
BF16 = mybir.dt.bfloat16
I32 = mybir.dt.int32
AF = mybir.ActivationFunctionType
ALU = mybir.AluOpType
AX = mybir.AxisListType

P = 128
D = 2048
S = 4096
NH = 16
HD = 128
ND = 16
OWN = 1024
NT = 8
NKA = 12
NKB = 28
NIE = 11
NIS = 44
NE = 16
QSCALE = float(1.0 / np.sqrt(128.0))
EPS = 1e-5

ENGS = ("tensor", "scalar", "vector", "sync", "gpsimd")
STREAM_SEM = {"tensor": "pe", "scalar": "act", "vector": "dve", "gpsimd": "pool"}


class Ctx:

    def __init__(self, nc, stack):
        self.nc = nc
        self.stack = stack
        self.ops = {e: [] for e in ENGS}
        self.cnt = {}
        self.sem = {}
        self.lastw = {}
        self.readers = {}
        self.waited = {e: {} for e in ENGS}
        self.dsems = set()
        for s in ("pe", "act", "dve", "pool"):
            self._mksem(s)

    def _mksem(self, name):
        if name not in self.sem:
            self.sem[name] = self.stack.enter_context(self.nc.semaphore(name))
            self.cnt[name] = 0
        return name

    def emit(self, eng, build, reads=(), writes=(), dma=False, pe_inc=True,
             dsem=None):
        if dma:
            key = dsem if dsem is not None else "D" + repr(writes[0])
            semname = self._mksem(key)
            self.dsems.add(semname)
            n = 16
            inc = True
        else:
            semname = STREAM_SEM[eng]
            n = 1
            inc = True
            if eng == "tensor":
                inc = bool(pe_inc)

        need = {}

        def _need(tok, waw_dma=False):
            if tok is None:
                return
            s, c = tok
            if s == "pe" and eng == "tensor" and not dma:
                return
            if waw_dma and s in self.dsems:
                return
            if c > self.cnt[s]:
                raise RuntimeError(
                    f"wait on future inc: {s}>={c} (emitted {self.cnt[s]}); "
                    f"a PE-read buffer is being overwritten before its "
                    f"accumulation group end was emitted")
            if need.get(s, 0) < c:
                need[s] = c

        for r in reads:
            for s, c in self.lastw.get(r, {}).items():
                _need((s, c))
        for w in writes:
            for s, c in self.lastw.get(w, {}).items():
                _need((s, c), waw_dma=dma)
            for s, c in self.readers.get(w, {}).items():
                _need((s, c))

        waits = []
        wd = self.waited[eng]
        for s, c in need.items():
            if wd.get(s, 0) < c:
                waits.append((s, c))
                wd[s] = c

        if inc:
            self.cnt[semname] += n
            tok = (semname, self.cnt[semname])
        else:
            tok = ("pe", self.cnt["pe"] + 1)

        for r in reads:
            d = self.readers.setdefault(r, {})
            if d.get(tok[0], 0) < tok[1]:
                d[tok[0]] = tok[1]
        for w in writes:
            d = self.lastw.setdefault(w, {})
            if d.get(tok[0], 0) < tok[1]:
                d[tok[0]] = tok[1]
            self.readers[w] = {}

        sem = self.sem
        semh = sem[semname]

        def closure(e, waits=waits, build=build, inc=inc, semh=semh, n=n):
            for s, c in waits:
                e.wait_ge(sem[s], c)
            ins = build(e)
            if inc:
                ins.then_inc(semh, n)

        self.ops[eng].append(closure)

    def barrier(self):
        snap = dict(self.cnt)
        sem = self.sem
        for eng in ENGS:
            wd = self.waited[eng]
            waits = []
            for s, c in snap.items():
                if c > 0 and wd.get(s, 0) < c:
                    waits.append((s, c))
                    wd[s] = c
            if waits:
                def closure(e, waits=waits):
                    for s, c in waits:
                        e.wait_ge(sem[s], c)
                self.ops[eng].append(closure)

    def final_waits(self):
        snap = dict(self.cnt)
        sem = self.sem

        def closure(e):
            for s, c in snap.items():
                if c > 0:
                    e.wait_ge(sem[s], c)
        self.ops["sync"].append(closure)

    def replay(self, block):
        for eng in ENGS:
            lst = self.ops[eng]
            if not lst:
                continue
            def runner(e, lst=lst):
                for f in lst:
                    f(e)
            getattr(block, eng)(runner)


def build_program():
    nc = bass.Bass("TRN2", target_bir_lowering=False)

    xfull = nc.dram_tensor("xfull", [S, D], F32, kind="ExternalInput")
    xown = nc.dram_tensor("xown", [OWN, D], F32, kind="ExternalInput")
    btab = nc.dram_tensor("btab", [P, 48], F32, kind="ExternalInput")
    kvidx = nc.dram_tensor("kvidx", [P, 10], I32, kind="ExternalInput")
    qwt = nc.dram_tensor("qwt", [D, D], F32, kind="ExternalInput")
    kwt = nc.dram_tensor("kwt", [D, D], F32, kind="ExternalInput")
    vwt = nc.dram_tensor("vwt", [D, D], F32, kind="ExternalInput")
    owt = nc.dram_tensor("owt", [D, D], F32, kind="ExternalInput")
    gwt = nc.dram_tensor("gwt", [HD, HD], F32, kind="ExternalInput")
    gateb = nc.dram_tensor("gateb", [P, 1], F32, kind="ExternalInput")
    gwsg = nc.dram_tensor("gwsg", [D, 17], F32, kind="ExternalInput")
    sw1 = nc.dram_tensor("sw1", [D, 5632], F32, kind="ExternalInput")
    sw3 = nc.dram_tensor("sw3", [D, 5632], F32, kind="ExternalInput")
    sw2 = nc.dram_tensor("sw2", [5632, D], F32, kind="ExternalInput")
    w1 = nc.dram_tensor("w1", [NE * D, 1408], F32, kind="ExternalInput")
    w3 = nc.dram_tensor("w3", [NE * D, 1408], F32, kind="ExternalInput")
    w2 = nc.dram_tensor("w2", [NE * 1408, D], F32, kind="ExternalInput")
    out = nc.dram_tensor("out", [OWN, D], F32, kind="ExternalOutput")

    ln0st = nc.dram_tensor("ln0st", [S, D], BF16, kind="Internal")
    lnfm = nc.dram_tensor("lnfm", [ND, P, S], BF16, kind="Internal")
    ownst = nc.dram_tensor("ownst", [OWN, D], BF16, kind="Internal")
    kdr = nc.dram_tensor("kdr", [NH, P, S], BF16, kind="Internal")
    vdr = nc.dram_tensor("vdr", [S, D], BF16, kind="Internal")
    ln1st = nc.dram_tensor("ln1st", [OWN, D], BF16, kind="Internal")
    x1dr = nc.dram_tensor("x1dr", [OWN, D], F32, kind="Internal")

    ctx_stack = ExitStack()
    with ctx_stack as stack:
        t = Ctx(nc, stack)

        SBUF_BASE = 16512

        def at(name, shape, dtype, off):
            return nc.alloc_sbuf_tensor_at(name, shape, dtype, offset=SBUF_BASE + off)

        pb = [stack.enter_context(nc.psum_tensor(f"pb{i}", [P, 512], F32))
              for i in range(8)]
        diagm = at("diagm", [P, 4, 512], BF16, 0)
        btab_sb = at("btab_sb", [P, 48], F32, 4096)
        onec = at("onec", [P, 1], BF16, 4288)
        oner = at("oner", [1, P], BF16, 208832)
        zr = at("zr", [1, 512], F32, 205760)
        zrb = at("zrb", [1, 512], BF16, 207808)
        kidx = at("kidx", [P, 10], I32, 4352)
        gb = at("gb", [P, 1], F32, 4416)
        gwt_sb = at("gwt_sb", [P, HD], BF16, 4448)
        gwsg_sb = at("gwsg_sb", [P, ND, 17], BF16, 4704)
        stat = at("stat", [P, 16], F32, 5280)
        cw = at("cw", [P, NT, 16], F32, 5344)
        sg = at("sg", [P, NT], F32, 5856)
        rsb = at("rsb", [P, 17], F32, 5888)
        r16 = at("r16", [P, 16], F32, 5984)
        mx8 = at("mx8", [P, 8], F32, 6048)
        xst = at("xst", [P, 2, D], F32, 6080)
        lntm = at("lntm", [P, 2, D], BF16, 22464)
        sqs = at("sqs", [P, D], F32, 30656)

        bst = at("bst", [P, 2, 7, NT], F32, 209600)
        SSQ, S1, NEGM, M2, VAR, LNV, RSTD = range(7)

        def ln_stats(src_ap, src_regs, tt, rp=0):
            t.emit("scalar",
                   lambda e, s=src_ap, tt=tt, rp=rp: e.activation(
                       sqs[:], s, AF.Square, accum_out=bst[:, rp, SSQ, tt:tt + 1]),
                   reads=list(src_regs), writes=["sqs", ("bst", rp, tt)])
            t.emit("vector",
                   lambda e, s=src_ap, tt=tt, rp=rp: e.reduce_sum(
                       bst[:, rp, S1, tt:tt + 1], s, axis=AX.X),
                   reads=list(src_regs), writes=[("bst", rp, tt)])

        def ln_chain(rp=0):
            rd = [("bst", rp, i) for i in range(NT)]
            t.emit("vector",
                   lambda e, rp=rp: e.tensor_scalar_mul(
                       bst[:, rp, NEGM, :], bst[:, rp, S1, :], -1.0 / D),
                   reads=rd, writes=[("bch", rp)] + rd)
            t.emit("vector",
                   lambda e, rp=rp: e.tensor_mul(
                       bst[:, rp, M2, :], bst[:, rp, NEGM, :], bst[:, rp, NEGM, :]),
                   reads=[("bch", rp)], writes=[("bch", rp)])
            t.emit("vector",
                   lambda e, rp=rp: e.scalar_tensor_tensor(
                       out=bst[:, rp, VAR, :], in0=bst[:, rp, SSQ, :], scalar=1.0 / D,
                       in1=bst[:, rp, M2, :], op0=ALU.mult, op1=ALU.subtract),
                   reads=[("bch", rp)], writes=[("bch", rp)])
            t.emit("vector",
                   lambda e, rp=rp: e.tensor_scalar_add(
                       bst[:, rp, VAR, :], bst[:, rp, VAR, :], EPS),
                   reads=[("bch", rp)], writes=[("bch", rp)])
            t.emit("scalar",
                   lambda e, rp=rp: e.activation(
                       bst[:, rp, LNV, :], bst[:, rp, VAR, :], AF.Ln),
                   reads=[("bch", rp)], writes=[("bch", rp)])
            t.emit("scalar",
                   lambda e, rp=rp: e.activation(
                       bst[:, rp, RSTD, :], bst[:, rp, LNV, :], AF.Exp, scale=-0.5),
                   reads=[("bch", rp)], writes=[("bch", rp)])

        def ln_apply(src_ap, src_regs, out_ap, out_regs, tt, rp=0):
            t.emit("vector",
                   lambda e, s=src_ap, o=out_ap, tt=tt, rp=rp: e.tensor_scalar(
                       out=o, in0=s, scalar1=bst[:, rp, NEGM, tt:tt + 1],
                       scalar2=bst[:, rp, RSTD, tt:tt + 1], op0=ALU.add, op1=ALU.mult),
                   reads=list(src_regs) + [("bch", rp)], writes=list(out_regs))

        def stcol(par, i):
            return stat[:, par * 8 + i: par * 8 + i + 1]

        def ln_tile(src_ap, src_regs, out_ap, out_regs, par, extra_reads=()):
            sreg = ("stat", par)
            qreg = "sqs"
            t.emit("scalar",
                   lambda e, s=src_ap, a=stcol(par, SSQ):
                       e.activation(sqs[:], s, AF.Square, accum_out=a),
                   reads=list(src_regs) + list(extra_reads),
                   writes=[qreg, sreg])
            t.emit("vector",
                   lambda e, s=src_ap, o=stcol(par, S1): e.reduce_sum(o, s, axis=AX.X),
                   reads=list(src_regs), writes=[sreg])
            t.emit("vector",
                   lambda e: e.tensor_scalar_mul(stcol(par, NEGM), stcol(par, S1), -1.0 / D),
                   reads=[sreg], writes=[sreg])
            t.emit("vector",
                   lambda e: e.tensor_mul(stcol(par, M2), stcol(par, NEGM), stcol(par, NEGM)),
                   reads=[sreg], writes=[sreg])
            t.emit("vector",
                   lambda e: e.scalar_tensor_tensor(
                       out=stcol(par, VAR), in0=stcol(par, SSQ), scalar=1.0 / D,
                       in1=stcol(par, M2), op0=ALU.mult, op1=ALU.subtract),
                   reads=[sreg], writes=[sreg])
            t.emit("vector",
                   lambda e: e.tensor_scalar_add(stcol(par, VAR), stcol(par, VAR), EPS),
                   reads=[sreg], writes=[sreg])
            t.emit("scalar",
                   lambda e: e.activation(stcol(par, LNV), stcol(par, VAR), AF.Ln),
                   reads=[sreg], writes=[sreg])
            t.emit("scalar",
                   lambda e: e.activation(stcol(par, RSTD), stcol(par, LNV), AF.Exp, scale=-0.5),
                   reads=[sreg], writes=[sreg])
            t.emit("vector",
                   lambda e, s=src_ap, o=out_ap: e.tensor_scalar(
                       out=o, in0=s, scalar1=stcol(par, NEGM), scalar2=stcol(par, RSTD),
                       op0=ALU.add, op1=ALU.mult),
                   reads=list(src_regs) + [sreg], writes=list(out_regs))

        for r in range(4):
            t.emit("gpsimd", lambda e, r=r: e.memset(diagm[:, r, :], 1.0),
                   writes=[("dm", r)])
            t.emit("gpsimd",
                   lambda e, r=r: e.affine_select(
                       out=diagm[:, r, :], in_=diagm[:, r, :],
                       compare_op=ALU.is_ge, fill=0.0, base=-128 * r,
                       pattern=[[1, 512]], channel_multiplier=-1),
                   reads=[("dm", r)], writes=[("dm", r)])
        t.emit("sync", lambda e: e.dma_start(btab_sb[:], btab[:]), writes=["btab"], dma=True)
        t.emit("gpsimd", lambda e: e.memset(onec[:], 1.0), writes=["onec"])
        t.emit("gpsimd", lambda e: e.memset(oner[:], 1.0), writes=["oner"])
        t.emit("sync", lambda e: e.dma_start(kidx[:], kvidx[:]), writes=["kidx"], dma=True)
        t.emit("sync", lambda e: e.dma_start(gb[:], gateb[:]), writes=["gb"], dma=True)
        t.emit("gpsimd", lambda e: e.dma_start(gwt_sb[:], gwt[:]), writes=["gwt"], dma=True)
        t.emit("gpsimd",
               lambda e: e.dma_start(gwsg_sb[:], gwsg.rearrange("(c p) e -> p c e", p=P)),
               writes=["gwsg"], dma=True)


        lnfb = [at("lnfb0", [P, ND, 1024], BF16, 38848),
                at("lnfb1", [P, ND, 1024], BF16, 71616)]
        kwres = at("kwres", [P, ND, NH, HD], BF16, 104384)
        kev = at("kev", [P, 2, 1024], BF16, 169920)
        vev = at("vev", [P, 2, D], BF16, 174016)

        t.emit("gpsimd",
               lambda e: e.dma_start(
                   kwres[:], kwt.rearrange("(c p) (h d) -> p c h d", p=P, d=HD)),
               writes=["kwres"], dma=True)
        xst2 = at("xst2", [P, 2, D], F32, 182208)

        def k_head(R, h):
            rp = R % 2
            bb = ((R * NH + h) % 2) * 2
            for c in range(ND):
                for g in range(2):
                    t.emit("tensor",
                           lambda e, c=c, g=g, h=h, bb=bb, rp=rp: e.matmul(
                               pb[bb + g][:, 0:512],
                               kwres[:, c, h, :],
                               lnfb[rp][:, c, g * 512:(g + 1) * 512],
                               start=(c == 0), stop=(c == ND - 1)),
                           reads=["kwres", ("lnfb", rp)],
                           writes=[f"pb{bb + g}"],
                           pe_inc=(c == ND - 1))
            kp = h % 2
            for g in range(2):
                t.emit("scalar",
                       lambda e, g=g, bb=bb, kp=kp: e.activation(
                           kev[:, kp, g * 512:(g + 1) * 512], pb[bb + g][:, 0:512],
                           AF.Copy, scale=QSCALE),
                       reads=[f"pb{bb + g}"], writes=[("kev", kp)])
            t.emit("sync",
                   lambda e, h=h, R=R, kp=kp: e.dma_start(
                       kdr[h, :, R * 1024:(R + 1) * 1024], kev[:, kp, :]),
                   reads=[("kev", kp)], writes=[("kdr", h)], dma=True,
                   dsem=f"d_kdr{kp}")

        for s in range(6):
            for i in range(8):
                if s >= 2:
                    k_head(s - 2, 2 * i)
                    k_head(s - 2, 2 * i + 1)
                if s < 4:
                    g = s * 8 + i
                    par = i % 2
                    t.emit("sync",
                           lambda e, g=g, par=par: e.dma_start(
                               xst[:, par, :], xfull[g * P:(g + 1) * P, :]),
                           writes=[("xst", par)], dma=True)
                    ln_stats(xst[:, par, :], [("xst", par)], i, rp=s % 2)
                if 1 <= s <= 4:
                    R = s - 1
                    g = R * 8 + i
                    par = i % 2
                    t.emit("sync",
                           lambda e, g=g, par=par: e.dma_start(
                               xst2[:, par, :], xfull[g * P:(g + 1) * P, :]),
                           writes=[("xs2", par)], dma=True)
                    ln_apply(xst2[:, par, :], [("xs2", par)],
                             lntm[:, par, :], [("lntm", par)], i, rp=R % 2)
                    t.emit("sync",
                           lambda e, g=g, par=par: e.dma_start(
                               ln0st[g * P:(g + 1) * P, :], lntm[:, par, :]),
                           reads=[("lntm", par)], writes=[("ln0st", R)], dma=True,
                           dsem=f"d_ln0st{par}")
            if s < 4:
                ln_chain(rp=s % 2)
            if 1 <= s <= 4:
                R = s - 1
                rp = R % 2
                for c in range(ND):
                    t.emit("scalar",
                           lambda e, R=R, c=c, rp=rp: e.dma_start(
                               lnfb[rp][:, c, :],
                               ln0st[R * 1024:(R + 1) * 1024, c * P:(c + 1) * P],
                               transpose=True),
                           reads=[("ln0st", R)], writes=[("lnfb", rp)], dma=True)
                t.emit("sync",
                       lambda e, R=R, rp=rp: e.dma_start(
                           lnfm.rearrange("c p t -> p c t")[:, :, R * 1024:(R + 1) * 1024],
                           lnfb[rp][:]),
                       reads=[("lnfb", rp)], writes=[("lnfm", R)], dma=True,
                       dsem=f"d_lnfm{rp}")
        t.barrier()

        vw_ap = kwres[:].rearrange("p c h d -> p c (h d)")
        t.emit("gpsimd",
               lambda e: e.dma_start(
                   vw_ap, vwt.rearrange("(c p) e -> p c e", p=P)),
               writes=["kwres"], dma=True)
        for R in range(4):
            rp = R % 2
            t.emit("sync",
                   lambda e, R=R, rp=rp: e.dma_start(
                       lnfb[rp][:],
                       lnfm.rearrange("c p t -> p c t")[:, :, R * 1024:(R + 1) * 1024]),
                   reads=[("lnfm", R)], writes=[("lnfb", rp)], dma=True)
            for tt in range(8):
                T = R * 8 + tt
                bb = (T % 2) * 4
                vp = T % 2
                for c in range(ND):
                    for eg in range(4):
                        t.emit("tensor",
                               lambda e, c=c, eg=eg, tt=tt, bb=bb, rp=rp: e.matmul(
                                   pb[bb + eg][:, 0:512],
                                   lnfb[rp][:, c, tt * P:(tt + 1) * P],
                                   vw_ap[:, c, eg * 512:(eg + 1) * 512],
                                   start=(c == 0), stop=(c == ND - 1)),
                               reads=["kwres", ("lnfb", rp)],
                               writes=[f"pb{bb + eg}"],
                               pe_inc=(c == ND - 1))
                for eg in range(4):
                    t.emit("scalar",
                           lambda e, eg=eg, bb=bb, vp=vp: e.activation(
                               vev[:, vp, eg * 512:(eg + 1) * 512],
                               pb[bb + eg][:, 0:512], AF.Copy),
                           reads=[f"pb{bb + eg}"], writes=[("vev", vp)])
                t.emit("sync",
                       lambda e, T=T, vp=vp: e.dma_start(
                           vdr[T * P:(T + 1) * P, :], vev[:, vp, :]),
                       reads=[("vev", vp)], writes=["v0"], dma=True,
                       dsem=f"d_v0{vp}")
        t.barrier()

        q_fm = at("q_fm", [P, NH, 1024], BF16, 38848)
        gate_all = at("gate_all", [P, NH, 1024], BF16, 71616)
        ownfm = at("ownfm", [P, ND, 1024], BF16, 104384)
        qwres = at("qwres", [P, ND, 8, HD], BF16, 137152)

        for tt in range(NT):
            par = tt % 2
            t.emit("sync",
                   lambda e, tt=tt, par=par: e.dma_start(
                       xst[:, par, :], xown[tt * P:(tt + 1) * P, :]),
                   writes=[("xst", par)], dma=True)
            ln_stats(xst[:, par, :], [("xst", par)], tt)
        ln_chain()
        for tt in range(NT):
            par = tt % 2
            t.emit("sync",
                   lambda e, tt=tt, par=par: e.dma_start(
                       xst[:, par, :], xown[tt * P:(tt + 1) * P, :]),
                   writes=[("xst", par)], dma=True)
            ln_apply(xst[:, par, :], [("xst", par)],
                     lntm[:, par, :], [("lntm", par)], tt)
            t.emit("sync",
                   lambda e, tt=tt, par=par: e.dma_start(
                       ownst[tt * P:(tt + 1) * P, :], lntm[:, par, :]),
                   reads=[("lntm", par)], writes=["ownst"], dma=True,
                   dsem=f"d_ownst{par}")
        for c in range(ND):
            t.emit("scalar",
                   lambda e, c=c: e.dma_start(
                       ownfm[:, c, :], ownst[:, c * P:(c + 1) * P], transpose=True),
                   reads=["ownst"], writes=["ownfm"], dma=True)
        for half in range(2):
            t.emit("gpsimd",
                   lambda e, half=half: e.dma_start(
                       qwres[:],
                       qwt.rearrange("(c p) (h d) -> p c h d", p=P, d=HD)[:, :, half * 8:(half + 1) * 8, :]),
                   writes=["qwres"], dma=True)
            for hh in range(8):
                h = half * 8 + hh
                bb = (h % 2) * 2
                for c in range(ND):
                    for g in range(2):
                        t.emit("tensor",
                               lambda e, c=c, g=g, hh=hh, bb=bb: e.matmul(
                                   pb[bb + g][:, 0:512],
                                   qwres[:, c, hh, :],
                                   ownfm[:, c, g * 512:(g + 1) * 512],
                                   start=(c == 0), stop=(c == ND - 1)),
                               reads=["qwres", "ownfm"],
                               writes=[f"pb{bb + g}"],
                               pe_inc=(c == ND - 1))
                for g in range(2):
                    t.emit("scalar",
                           lambda e, h=h, g=g, bb=bb: e.activation(
                               q_fm[:, h, g * 512:(g + 1) * 512],
                               pb[bb + g][:, 0:512], AF.Copy),
                           reads=[f"pb{bb + g}"], writes=["qfm"])
        for h in range(NH):
            bb = (h % 2) * 2
            for g in range(2):
                t.emit("tensor",
                       lambda e, h=h, g=g, bb=bb: e.matmul(
                           pb[bb + g][:, 0:512], gwt_sb[:],
                           q_fm[:, h, g * 512:(g + 1) * 512],
                           start=True, stop=True),
                       reads=["gwt", "qfm"], writes=[f"pb{bb + g}"], pe_inc=True)
                t.emit("scalar",
                       lambda e, h=h, g=g, bb=bb: e.activation(
                           gate_all[:, h, g * 512:(g + 1) * 512],
                           pb[bb + g][:, 0:512], AF.Sigmoid, bias=gb[:]),
                       reads=[f"pb{bb + g}", "gb"], writes=["gate"])
        t.barrier()

        gav = at("gav", [P, NH, 1024], BF16, 104384)
        kbulk = [at("kbulk0", [P, NKB * P], BF16, 137152),
                 at("kbulk1", [P, NKB * P], BF16, 144320)]
        vbulk = [at("vbulk0", [P, NKB, P], BF16, 151488),
                 at("vbulk1", [P, NKB, P], BF16, 158656)]
        kdg = [at("kdg0", [P, 512], BF16, 165824),
               at("kdg1", [P, 512], BF16, 166848)]
        vdg = [at("vdg0", [P, 4, P], BF16, 167872),
               at("vdg1", [P, 4, P], BF16, 168896)]
        expt = [at("expt0", [P, 512], BF16, 169920),
                at("expt1", [P, 512], BF16, 170944),
                at("expt2", [P, 512], BF16, 174016),
                at("expt3", [P, 512], BF16, 175040)]
        avsb = [at("avsb0", [P, 512], BF16, 171968),
                at("avsb1", [P, 512], BF16, 172992)]

        kdr_blocks = kdr.rearrange("h p (b t) -> (h p b) t", t=P)
        job = 0
        for g in range(2):
            n = NKA if g == 0 else NKB
            boff = 0 if g == 0 else NKA
            for h in range(NH):
                jp = job % 2
                avb = 4 + (job % 2)
                zb = 6 + (job % 2)
                t.emit("sync",
                       lambda e, h=h, n=n, jp=jp: e.dma_start(
                           kbulk[jp][:, 0:n * P], kdr[h, :, 0:n * P]),
                       reads=[("kdr", h)], writes=[("kbulk", jp)], dma=True)
                t.emit("sync",
                       lambda e, h=h, n=n, jp=jp: e.dma_start(
                           vbulk[jp][:, 0:n, :],
                           vdr.rearrange("(b p) e -> p b e", p=P)[:, 0:n, h * HD:(h + 1) * HD]),
                       reads=["v0"], writes=[("vbulk", jp)], dma=True)
                t.emit("gpsimd",
                       lambda e, h=h, g=g, jp=jp: e.indirect_dma_start(
                           kdg[jp][:], None, kdr_blocks[:, :],
                           IndirectOffsetOnAxis(kidx[:, g:g + 1], 0),
                           element_offset=h * P * S),
                       reads=[("kdr", h), "kidx"], writes=[("kdg", jp)], dma=True)
                for i in range(4):
                    t.emit("gpsimd",
                           lambda e, h=h, g=g, i=i, jp=jp: e.indirect_dma_start(
                               vdg[jp][:, i, :], None, vdr[:, :],
                               IndirectOffsetOnAxis(kidx[:, 2 + g * 4 + i: 3 + g * 4 + i], 0),
                               element_offset=h * HD),
                           reads=["v0", "kidx"], writes=[("vdg", jp)], dma=True)
                qslice = q_fm[:, h, g * 512:(g + 1) * 512]

                def sc_stage(s):
                    sp = s % 4
                    diag = s >= n
                    i = s - n
                    if not diag:
                        klhs = (lambda e, s=s, jp=jp: kbulk[jp][:, s * P:(s + 1) * P])
                        kreads = [("kbulk", jp)]
                    else:
                        klhs = (lambda e, i=i, jp=jp: kdg[jp][:, i * P:(i + 1) * P])
                        kreads = [("kdg", jp)]
                    t.emit("tensor",
                           lambda e, klhs=klhs, qs=qslice, sp=sp: e.matmul(
                               pb[sp][:, 0:512], klhs(e), qs,
                               start=True, stop=True),
                           reads=kreads + ["qfm"], writes=[f"pb{sp}"],
                           pe_inc=True)
                    bcol = 40 if diag else boff + s
                    t.emit("scalar",
                           lambda e, sp=sp, bcol=bcol: e.activation(
                               expt[sp][:], pb[sp][:, 0:512], AF.Exp,
                               bias=btab_sb[:, bcol:bcol + 1]),
                           reads=[f"pb{sp}", "btab"], writes=[("expt", sp)])
                    if diag:
                        t.emit("vector",
                               lambda e, sp=sp, i=i: e.tensor_mul(
                                   expt[sp][:], expt[sp][:], diagm[:, i, :]),
                               reads=[("expt", sp), ("dm", i)], writes=[("expt", sp)])

                def av_stage(s):
                    sp = s % 4
                    diag = s >= n
                    i = s - n
                    if not diag:
                        vlhs = (lambda e, s=s, jp=jp: vbulk[jp][:, s, :])
                        vreads = [("vbulk", jp)]
                    else:
                        vlhs = (lambda e, i=i, jp=jp: vdg[jp][:, i, :])
                        vreads = [("vdg", jp)]
                    t.emit("tensor",
                           lambda e, vlhs=vlhs, sp=sp, avb=avb, s=s, last=(s == n + 3): e.matmul(
                               pb[avb][:, 0:512], vlhs(e), expt[sp][:],
                               start=(s == 0), stop=last),
                           reads=vreads + [("expt", sp)], writes=[f"pb{avb}"],
                           pe_inc=(s == n + 3))
                    t.emit("tensor",
                           lambda e, sp=sp, zb=zb, s=s, last=(s == n + 3): e.matmul(
                               pb[zb][0:1, 0:512], onec[:], expt[sp][:],
                               start=(s == 0), stop=last),
                           reads=["onec", ("expt", sp)], writes=[f"pb{zb}"],
                           pe_inc=(s == n + 3))

                LAG = 2
                for s in range(n + 4):
                    sc_stage(s)
                    if s >= LAG:
                        av_stage(s - LAG)
                for s in range(n + 4 - LAG, n + 4):
                    av_stage(s)

                t.emit("scalar",
                       lambda e, zb=zb: e.activation(zr[:], pb[zb][0:1, 0:512], AF.Ln),
                       reads=[f"pb{zb}"], writes=["zr"])
                t.emit("scalar",
                       lambda e: e.activation(zrb[:], zr[:], AF.Exp, scale=-1.0),
                       reads=["zr"], writes=["zrb"])
                bbc = (n + 4) % 4
                t.emit("tensor",
                       lambda e, bbc=bbc: e.matmul(
                           pb[bbc][:, 0:512], oner[:], zrb[:],
                           start=True, stop=True),
                       reads=["oner", "zrb"], writes=[f"pb{bbc}"], pe_inc=True)
                t.emit("scalar",
                       lambda e, avb=avb, jp=jp: e.activation(
                           avsb[jp][:], pb[avb][:, 0:512], AF.Copy),
                       reads=[f"pb{avb}"], writes=[("avsb", jp)])
                t.emit("vector",
                       lambda e, h=h, g=g, jp=jp: e.tensor_mul(
                           avsb[jp][:], avsb[jp][:],
                           gate_all[:, h, g * 512:(g + 1) * 512]),
                       reads=[("avsb", jp), "gate"], writes=[("avsb", jp)])
                t.emit("vector",
                       lambda e, h=h, g=g, jp=jp, bbc=bbc: e.tensor_mul(
                           gav[:, h, g * 512:(g + 1) * 512], avsb[jp][:],
                           pb[bbc][:, 0:512]),
                       reads=[("avsb", jp), f"pb{bbc}"], writes=["gav"])
                job += 1
        t.barrier()

        lnf1 = at("lnf1", [P, ND, 1024], BF16, 38848)
        owres = at("owres", [P, ND, D], BF16, 137152)
        aosb = at("aosb", [P, D], F32, 71616)
        ansb = at("ansb", [P, D], F32, 79808)

        t.emit("gpsimd",
               lambda e: e.dma_start(owres[:], owt.rearrange("(c p) e -> p c e", p=P)),
               writes=["owres"], dma=True)
        for tt in range(NT):
            bb = (tt % 2) * 4
            par = tt % 2
            for c in range(ND):
                for eg in range(4):
                    t.emit("tensor",
                           lambda e, c=c, eg=eg, tt=tt, bb=bb: e.matmul(
                               pb[bb + eg][:, 0:512],
                               gav[:, c, tt * P:(tt + 1) * P],
                               owres[:, c, eg * 512:(eg + 1) * 512],
                               start=(c == 0), stop=(c == ND - 1)),
                           reads=["gav", "owres"], writes=[f"pb{bb + eg}"],
                           pe_inc=(c == ND - 1))
            for eg in range(4):
                t.emit("scalar",
                       lambda e, eg=eg, bb=bb: e.activation(
                           aosb[:, eg * 512:(eg + 1) * 512], pb[bb + eg][:, 0:512],
                           AF.Copy),
                       reads=[f"pb{bb + eg}"], writes=["aosb"])
            ln_tile(aosb[:], ["aosb"], ansb[:], ["ansb"], par)
            t.emit("sync",
                   lambda e, tt=tt, par=par: e.dma_start(
                       xst[:, par, :], xown[tt * P:(tt + 1) * P, :]),
                   writes=[("xst", par)], dma=True)
            t.emit("vector",
                   lambda e, par=par: e.tensor_add(ansb[:], ansb[:], xst[:, par, :]),
                   reads=["ansb", ("xst", par)], writes=["ansb"])
            t.emit("sync",
                   lambda e, tt=tt: e.dma_start(x1dr[tt * P:(tt + 1) * P, :], ansb[:]),
                   reads=["ansb"], writes=["x1dr"], dma=True)
            ln_tile(ansb[:], ["ansb"], lntm[:, par, :], [("lntm", par)], par)
            t.emit("sync",
                   lambda e, tt=tt, par=par: e.dma_start(
                       ln1st[tt * P:(tt + 1) * P, :], lntm[:, par, :]),
                   reads=[("lntm", par)], writes=["ln1st"], dma=True,
                   dsem=f"d_ln1st{par}")
        for c in range(ND):
            t.emit("scalar",
                   lambda e, c=c: e.dma_start(
                       lnf1[:, c, :], ln1st[:, c * P:(c + 1) * P], transpose=True),
                   reads=["ln1st"], writes=["lnf1"], dma=True)
        t.barrier()

        for tt in range(NT):
            bk = tt % 2
            for c in range(ND):
                t.emit("tensor",
                       lambda e, c=c, tt=tt, bk=bk: e.matmul(
                           pb[bk][:, 0:17],
                           lnf1[:, c, tt * P:(tt + 1) * P], gwsg_sb[:, c, :],
                           start=(c == 0), stop=(c == ND - 1)),
                       reads=["lnf1", "gwsg"], writes=[f"pb{bk}"],
                       pe_inc=(c == ND - 1))
            t.emit("scalar",
                   lambda e, bk=bk: e.activation(rsb[:], pb[bk][:, 0:17], AF.Copy),
                   reads=[f"pb{bk}"], writes=["rsb"])
            t.emit("vector", lambda e: e.reduce_max(stat[:, 0:1], rsb[:, 0:16], axis=AX.X),
                   reads=["rsb"], writes=[("stat", 0)])
            t.emit("vector", lambda e: e.tensor_scalar_mul(stat[:, 1:2], stat[:, 0:1], -1.0),
                   reads=[("stat", 0)], writes=[("stat", 0)])
            t.emit("scalar",
                   lambda e: e.activation(r16[:], rsb[:, 0:16], AF.Exp,
                                          bias=stat[:, 1:2], accum_out=stat[:, 2:3]),
                   reads=["rsb", ("stat", 0)], writes=["r16", ("stat", 0)])
            t.emit("vector", lambda e: e.reciprocal(stat[:, 3:4], stat[:, 2:3]),
                   reads=[("stat", 0)], writes=[("stat", 0)])
            t.emit("vector",
                   lambda e: e.tensor_scalar_mul(r16[:], r16[:], stat[:, 3:4]),
                   reads=["r16", ("stat", 0)], writes=["r16"])
            t.emit("vector", lambda e: e.max(out=mx8[:], in_=r16[:]),
                   reads=["r16"], writes=["mx8"])
            t.emit("vector", lambda e: e.memset(mx8[:, 4:8], 0.0),
                   reads=["mx8"], writes=["mx8"])
            t.emit("vector",
                   lambda e: e.match_replace(out=rsb[:, 0:16], in_to_replace=mx8[:],
                                             in_values=r16[:], imm_value=0.0),
                   reads=["mx8", "r16"], writes=["rsb"])
            t.emit("vector",
                   lambda e, tt=tt: e.tensor_sub(cw[:, tt, :], r16[:], rsb[:, 0:16]),
                   reads=["r16", "rsb"], writes=["cw"])
            t.emit("scalar",
                   lambda e: e.activation(stat[:, 4:5], rsb[:, 16:17], AF.Exp, scale=-1.0),
                   reads=["rsb"], writes=[("stat", 0)])
            t.emit("vector",
                   lambda e: e.tensor_scalar_add(stat[:, 4:5], stat[:, 4:5], 1.0),
                   reads=[("stat", 0)], writes=[("stat", 0)])
            t.emit("vector",
                   lambda e, tt=tt: e.reciprocal(sg[:, tt:tt + 1], stat[:, 4:5]),
                   reads=[("stat", 0)], writes=["sg"])
        t.barrier()

        acc = at("acc", [P, NT, D], F32, 71616)
        he = at("he", [P, NIE, 1024], BF16, 137152)
        w1r = [at("w1r0", [P, ND, P], BF16, 159680),
               at("w1r1", [P, ND, P], BF16, 163776)]
        w3r = [at("w3r0", [P, ND, P], BF16, 167872),
               at("w3r1", [P, ND, P], BF16, 171968)]
        w2p = [at("w2p0", [P, NIE, 512], BF16, 176064),
               at("w2p1", [P, NIE, 512], BF16, 187328)]
        ssc = at("ssc", [P, 2, 512], F32, 198592)

        def expert(widx, first):
            kind, ix = widx
            if kind == "s":
                w1src = sw1.rearrange("(c p) i -> p c i", p=P)
                w3src = sw3.rearrange("(c p) i -> p c i", p=P)
                w2src = sw2.rearrange("(c p) e -> p c e", p=P)
                icoff = ix * NIE
            else:
                w1src = w1[ix * D:(ix + 1) * D, :].rearrange("(c p) i -> p c i", p=P)
                w3src = w3[ix * D:(ix + 1) * D, :].rearrange("(c p) i -> p c i", p=P)
                w2src = w2[ix * 1408:(ix + 1) * 1408, :].rearrange("(c p) e -> p c e", p=P)
                icoff = 0
            for ic in range(NIE):
                wp = ic % 2
                t.emit("gpsimd",
                       lambda e, ic=ic, wp=wp, w1src=w1src: e.dma_start(
                           w1r[wp][:],
                           w1src[:, :, (icoff + ic) * P:(icoff + ic + 1) * P]),
                       writes=[("w1r", wp)], dma=True)
                t.emit("gpsimd",
                       lambda e, ic=ic, wp=wp, w3src=w3src: e.dma_start(
                           w3r[wp][:],
                           w3src[:, :, (icoff + ic) * P:(icoff + ic + 1) * P]),
                       writes=[("w3r", wp)], dma=True)
                bb = (ic % 2) * 4
                for c in range(ND):
                    for g in range(2):
                        t.emit("tensor",
                               lambda e, c=c, g=g, wp=wp, bb=bb: e.matmul(
                                   pb[bb + g][:, 0:512], w1r[wp][:, c, :],
                                   lnf1[:, c, g * 512:(g + 1) * 512],
                                   start=(c == 0), stop=(c == ND - 1)),
                               reads=[("w1r", wp), "lnf1"], writes=[f"pb{bb + g}"],
                               pe_inc=(c == ND - 1))
                for c in range(ND):
                    for g in range(2):
                        t.emit("tensor",
                               lambda e, c=c, g=g, wp=wp, bb=bb: e.matmul(
                                   pb[bb + 2 + g][:, 0:512], w3r[wp][:, c, :],
                                   lnf1[:, c, g * 512:(g + 1) * 512],
                                   start=(c == 0), stop=(c == ND - 1)),
                               reads=[("w3r", wp), "lnf1"], writes=[f"pb{bb + 2 + g}"],
                               pe_inc=(c == ND - 1))
                for g in range(2):
                    t.emit("scalar",
                           lambda e, g=g, bb=bb: e.activation(
                               ssc[:, g, :], pb[bb + g][:, 0:512], AF.Silu),
                           reads=[f"pb{bb + g}"], writes=[("ssc", g)])
                    t.emit("vector",
                           lambda e, g=g, ic=ic, bb=bb: e.tensor_mul(
                               he[:, ic, g * 512:(g + 1) * 512], ssc[:, g, :],
                               pb[bb + 2 + g][:, 0:512]),
                           reads=[("ssc", g), f"pb{bb + 2 + g}"],
                           writes=[("he", ic)])
            for eg in range(4):
                wp = eg % 2
                t.emit("gpsimd",
                       lambda e, eg=eg, wp=wp, w2src=w2src: e.dma_start(
                           w2p[wp][:],
                           w2src[:, icoff:icoff + NIE, eg * 512:(eg + 1) * 512]),
                       writes=[("w2p", wp)], dma=True)
                for tp in range(4):
                    bb = (tp % 2) * 2
                    for ic in range(NIE):
                        for i in range(2):
                            t.emit("tensor",
                                   lambda e, ic=ic, i=i, tp=tp, wp=wp, bb=bb: e.matmul(
                                       pb[bb + i][:, 0:512],
                                       he[:, ic, (tp * 2 + i) * P:(tp * 2 + i + 1) * P],
                                       w2p[wp][:, ic, :],
                                       start=(ic == 0), stop=(ic == NIE - 1)),
                                   reads=[("he", ic), ("w2p", wp)],
                                   writes=[f"pb{bb + i}"],
                                   pe_inc=(ic == NIE - 1))
                    for i in range(2):
                        tt = tp * 2 + i
                        scal = (lambda e, tt=tt: sg[:, tt:tt + 1]) if kind == "s" \
                            else (lambda e, tt=tt, ix=ix: cw[:, tt, ix:ix + 1])
                        sreads = ["sg"] if kind == "s" else ["cw"]
                        if first:
                            t.emit("vector",
                                   lambda e, tt=tt, eg=eg, i=i, bb=bb, scal=scal:
                                       e.tensor_scalar_mul(
                                           acc[:, tt, eg * 512:(eg + 1) * 512],
                                           pb[bb + i][:, 0:512], scal(e)),
                                   reads=[f"pb{bb + i}"] + sreads,
                                   writes=[("acc", tt)])
                        else:
                            t.emit("vector",
                                   lambda e, tt=tt, eg=eg, i=i, bb=bb, scal=scal:
                                       e.scalar_tensor_tensor(
                                           out=acc[:, tt, eg * 512:(eg + 1) * 512],
                                           in0=pb[bb + i][:, 0:512], scalar=scal(e),
                                           in1=acc[:, tt, eg * 512:(eg + 1) * 512],
                                           op0=ALU.mult, op1=ALU.add),
                                   reads=[f"pb{bb + i}", ("acc", tt)] + sreads,
                                   writes=[("acc", tt)])

        for q in range(4):
            expert(("s", q), first=(q == 0))
        for e_ in range(NE):
            expert(("e", e_), first=False)
        t.barrier()

        jns = at("jns", [P, 2, D], F32, 137152)
        for tt in range(NT):
            ln_stats(acc[:, tt, :], [("acc", tt)], tt)
        ln_chain()
        for tt in range(NT):
            par = tt % 2
            ln_apply(acc[:, tt, :], [("acc", tt)], jns[:, par, :], [("jns", par)], tt)
            t.emit("sync",
                   lambda e, tt=tt, par=par: e.dma_start(
                       xst[:, par, :], x1dr[tt * P:(tt + 1) * P, :]),
                   reads=["x1dr"], writes=[("xst", par)], dma=True)
            t.emit("vector",
                   lambda e, par=par: e.tensor_add(
                       jns[:, par, :], jns[:, par, :], xst[:, par, :]),
                   reads=[("jns", par), ("xst", par)], writes=[("jns", par)])
            t.emit("sync",
                   lambda e, tt=tt, par=par: e.dma_start(
                       out[tt * P:(tt + 1) * P, :], jns[:, par, :]),
                   reads=[("jns", par)], writes=["outw"], dma=True,
                   dsem=f"d_out{par}")
        t.final_waits()

        block = stack.enter_context(nc.Block())
        t.replay(block)
    return nc


def kernel(L0_attn_x, L0_attn_q_w, L0_attn_k_w, L0_attn_v_w, L0_attn_o_w,
           L0_attn_gate_w, L0_attn_gate_b, L0_ffn_gate_w, L0_ffn_w1, L0_ffn_w3,
           L0_ffn_w2, L0_ffn_s_w1, L0_ffn_s_w3, L0_ffn_s_w2, L0_ffn_sg_w):
    x = np.ascontiguousarray(np.asarray(L0_attn_x, dtype=np.float32))
    B = x.shape[0]
    shared = {
        "qwt": np.ascontiguousarray(np.asarray(L0_attn_q_w, np.float32).T),
        "kwt": np.ascontiguousarray(np.asarray(L0_attn_k_w, np.float32).T),
        "vwt": np.ascontiguousarray(np.asarray(L0_attn_v_w, np.float32).T),
        "owt": np.ascontiguousarray(np.asarray(L0_attn_o_w, np.float32).T),
        "gwt": np.ascontiguousarray(np.asarray(L0_attn_gate_w, np.float32).T),
        "gateb": np.ascontiguousarray(np.asarray(L0_attn_gate_b, np.float32).reshape(P, 1)),
        "gwsg": np.ascontiguousarray(np.concatenate(
            [np.asarray(L0_ffn_gate_w, np.float32),
             np.asarray(L0_ffn_sg_w, np.float32)], axis=1)),
        "sw1": np.ascontiguousarray(np.asarray(L0_ffn_s_w1, np.float32)),
        "sw3": np.ascontiguousarray(np.asarray(L0_ffn_s_w3, np.float32)),
        "sw2": np.ascontiguousarray(np.asarray(L0_ffn_s_w2, np.float32)),
        "w1": np.ascontiguousarray(np.asarray(L0_ffn_w1, np.float32).reshape(NE * D, 1408)),
        "w3": np.ascontiguousarray(np.asarray(L0_ffn_w3, np.float32).reshape(NE * D, 1408)),
        "w2": np.ascontiguousarray(np.asarray(L0_ffn_w2, np.float32).reshape(NE * 1408, D)),
    }

    in_maps = []
    for c in range(8):
        b, j = c // 4, c % 4
        a, bb = j, 7 - j
        xown = np.ascontiguousarray(np.concatenate(
            [x[b, 512 * a:512 * (a + 1)], x[b, 512 * bb:512 * (bb + 1)]], axis=0))
        btab = np.full((P, 48), -1e30, np.float32)
        for s in range(NKA):
            if s < 4 * a:
                btab[:, s] = 14.0
        for s in range(NKB):
            if s < 4 * bb:
                btab[:, NKA + s] = 14.0
        btab[:, 40] = 14.0
        kvidx = np.zeros((P, 10), np.int32)
        pr = np.arange(P, dtype=np.int32)
        kvidx[:, 0] = pr * 32 + 4 * a
        kvidx[:, 1] = pr * 32 + 4 * bb
        for i in range(4):
            kvidx[:, 2 + i] = (4 * a + i) * P + pr
            kvidx[:, 6 + i] = (4 * bb + i) * P + pr
        m = {"xfull": x[b], "xown": xown,
             "btab": btab, "kvidx": kvidx}
        m.update(shared)
        in_maps.append(m)

    nc = build_program()
    res = run_bass_kernel_spmd(nc, in_maps, core_ids=list(range(8)))

    y = np.empty((B, S, D), np.float32)
    for c in range(8):
        b, j = c // 4, c % 4
        a, bb = j, 7 - j
        o = res.results[c]["out"]
        y[b, 512 * a:512 * (a + 1)] = o[0:512]
        y[b, 512 * bb:512 * (bb + 1)] = o[512:1024]
    return y


# version 16 (final)
# speedup vs baseline: 1.1100x; 1.0184x over previous
import numpy as np
import ml_dtypes
from contextlib import ExitStack

import concourse.bass as bass
import concourse.mybir as mybir
from concourse.bass import IndirectOffsetOnAxis
from concourse.bass_utils import run_bass_kernel_spmd

F32 = mybir.dt.float32
BF16 = mybir.dt.bfloat16
I32 = mybir.dt.int32
AF = mybir.ActivationFunctionType
ALU = mybir.AluOpType
AX = mybir.AxisListType

P = 128
D = 2048
S = 4096
NH = 16
HD = 128
ND = 16
OWN = 1024
NT = 8
NKA = 12
NKB = 28
NIE = 11
NIS = 44
NE = 16
QSCALE = float(1.0 / np.sqrt(128.0))
EPS = 1e-5

ENGS = ("tensor", "scalar", "vector", "sync", "gpsimd")
STREAM_SEM = {"tensor": "pe", "scalar": "act", "vector": "dve", "gpsimd": "pool"}


class Ctx:

    def __init__(self, nc, stack):
        self.nc = nc
        self.stack = stack
        self.ops = {e: [] for e in ENGS}
        self.cnt = {}
        self.sem = {}
        self.lastw = {}
        self.readers = {}
        self.waited = {e: {} for e in ENGS}
        self.dsems = set()
        for s in ("pe", "act", "dve", "pool"):
            self._mksem(s)

    def _mksem(self, name):
        if name not in self.sem:
            self.sem[name] = self.stack.enter_context(self.nc.semaphore(name))
            self.cnt[name] = 0
        return name

    def emit(self, eng, build, reads=(), writes=(), dma=False, pe_inc=True,
             dsem=None):
        if dma:
            key = dsem if dsem is not None else "D" + repr(writes[0])
            semname = self._mksem(key)
            self.dsems.add(semname)
            n = 16
            inc = True
        else:
            semname = STREAM_SEM[eng]
            n = 1
            inc = True
            if eng == "tensor":
                inc = bool(pe_inc)

        need = {}

        def _need(tok, waw_dma=False):
            if tok is None:
                return
            s, c = tok
            if s == "pe" and eng == "tensor" and not dma:
                return
            if waw_dma and s in self.dsems:
                return
            if c > self.cnt[s]:
                raise RuntimeError(
                    f"wait on future inc: {s}>={c} (emitted {self.cnt[s]}); "
                    f"a PE-read buffer is being overwritten before its "
                    f"accumulation group end was emitted")
            if need.get(s, 0) < c:
                need[s] = c

        for r in reads:
            for s, c in self.lastw.get(r, {}).items():
                _need((s, c))
        for w in writes:
            for s, c in self.lastw.get(w, {}).items():
                _need((s, c), waw_dma=dma)
            for s, c in self.readers.get(w, {}).items():
                _need((s, c))

        waits = []
        wd = self.waited[eng]
        for s, c in need.items():
            if wd.get(s, 0) < c:
                waits.append((s, c))
                wd[s] = c

        if inc:
            self.cnt[semname] += n
            tok = (semname, self.cnt[semname])
        else:
            tok = ("pe", self.cnt["pe"] + 1)

        for r in reads:
            d = self.readers.setdefault(r, {})
            if d.get(tok[0], 0) < tok[1]:
                d[tok[0]] = tok[1]
        for w in writes:
            d = self.lastw.setdefault(w, {})
            if d.get(tok[0], 0) < tok[1]:
                d[tok[0]] = tok[1]
            self.readers[w] = {}

        sem = self.sem
        semh = sem[semname]

        def closure(e, waits=waits, build=build, inc=inc, semh=semh, n=n):
            for s, c in waits:
                e.wait_ge(sem[s], c)
            ins = build(e)
            if inc:
                ins.then_inc(semh, n)

        self.ops[eng].append(closure)

    def barrier(self):
        snap = dict(self.cnt)
        sem = self.sem
        for eng in ENGS:
            wd = self.waited[eng]
            waits = []
            for s, c in snap.items():
                if c > 0 and wd.get(s, 0) < c:
                    waits.append((s, c))
                    wd[s] = c
            if waits:
                def closure(e, waits=waits):
                    for s, c in waits:
                        e.wait_ge(sem[s], c)
                self.ops[eng].append(closure)

    def final_waits(self):
        snap = dict(self.cnt)
        sem = self.sem

        def closure(e):
            for s, c in snap.items():
                if c > 0:
                    e.wait_ge(sem[s], c)
        self.ops["sync"].append(closure)

    def replay(self, block):
        for eng in ENGS:
            lst = self.ops[eng]
            if not lst:
                continue
            def runner(e, lst=lst):
                for f in lst:
                    f(e)
            getattr(block, eng)(runner)


def build_program():
    nc = bass.Bass("TRN2", target_bir_lowering=False)

    xfull = nc.dram_tensor("xfull", [S, D], F32, kind="ExternalInput")
    xown = nc.dram_tensor("xown", [OWN, D], F32, kind="ExternalInput")
    btab = nc.dram_tensor("btab", [P, 48], F32, kind="ExternalInput")
    kvidx = nc.dram_tensor("kvidx", [P, 10], I32, kind="ExternalInput")
    qwt = nc.dram_tensor("qwt", [D, D], F32, kind="ExternalInput")
    kwt = nc.dram_tensor("kwt", [D, D], F32, kind="ExternalInput")
    vwt = nc.dram_tensor("vwt", [D, D], F32, kind="ExternalInput")
    owt = nc.dram_tensor("owt", [D, D], F32, kind="ExternalInput")
    gwt = nc.dram_tensor("gwt", [HD, HD], F32, kind="ExternalInput")
    gateb = nc.dram_tensor("gateb", [P, 1], F32, kind="ExternalInput")
    gwsg = nc.dram_tensor("gwsg", [D, 17], F32, kind="ExternalInput")
    sw1 = nc.dram_tensor("sw1", [D, 5632], F32, kind="ExternalInput")
    sw3 = nc.dram_tensor("sw3", [D, 5632], F32, kind="ExternalInput")
    sw2 = nc.dram_tensor("sw2", [5632, D], F32, kind="ExternalInput")
    w1 = nc.dram_tensor("w1", [NE * D, 1408], F32, kind="ExternalInput")
    w3 = nc.dram_tensor("w3", [NE * D, 1408], F32, kind="ExternalInput")
    w2 = nc.dram_tensor("w2", [NE * 1408, D], F32, kind="ExternalInput")
    out = nc.dram_tensor("out", [OWN, D], F32, kind="ExternalOutput")

    ln0st = nc.dram_tensor("ln0st", [S, D], BF16, kind="Internal")
    lnfm = nc.dram_tensor("lnfm", [ND, P, S], BF16, kind="Internal")
    ownst = nc.dram_tensor("ownst", [OWN, D], BF16, kind="Internal")
    kdr = nc.dram_tensor("kdr", [NH, P, S], BF16, kind="Internal")
    vdr = nc.dram_tensor("vdr", [S, D], BF16, kind="Internal")
    ln1st = nc.dram_tensor("ln1st", [OWN, D], BF16, kind="Internal")
    x1dr = nc.dram_tensor("x1dr", [OWN, D], F32, kind="Internal")

    ctx_stack = ExitStack()
    with ctx_stack as stack:
        t = Ctx(nc, stack)

        SBUF_BASE = 16512

        def at(name, shape, dtype, off):
            return nc.alloc_sbuf_tensor_at(name, shape, dtype, offset=SBUF_BASE + off)

        pb = [stack.enter_context(nc.psum_tensor(f"pb{i}", [P, 512], F32))
              for i in range(8)]
        diagm = at("diagm", [P, 4, 512], BF16, 0)
        btab_sb = at("btab_sb", [P, 48], F32, 4096)
        onec = at("onec", [P, 1], BF16, 4288)
        oner = at("oner", [1, P], BF16, 210880)
        zr = at("zr", [1, 2, 512], F32, 204736)
        zrb = at("zrb", [1, 2, 512], BF16, 208832)
        kidx = at("kidx", [P, 10], I32, 4352)
        gb = at("gb", [P, 1], F32, 4416)
        gwt_sb = at("gwt_sb", [P, HD], BF16, 4448)
        gwsg_sb = at("gwsg_sb", [P, ND, 17], BF16, 4704)
        stat = at("stat", [P, 16], F32, 5280)
        cw = at("cw", [P, NT, 16], F32, 5344)
        sg = at("sg", [P, NT], F32, 5856)
        rsb = at("rsb", [P, 17], F32, 5888)
        r16 = at("r16", [P, 16], F32, 5984)
        mx8 = at("mx8", [P, 8], F32, 6048)
        xst = at("xst", [P, 2, D], F32, 6080)
        lntm = at("lntm", [P, 2, D], BF16, 22464)
        sqs = at("sqs", [P, D], F32, 30656)

        bst = at("bst", [P, 2, 7, NT], F32, 209600)
        SSQ, S1, NEGM, M2, VAR, LNV, RSTD = range(7)

        def ln_stats(src_ap, src_regs, tt, rp=0):
            t.emit("scalar",
                   lambda e, s=src_ap, tt=tt, rp=rp: e.activation(
                       sqs[:], s, AF.Square, accum_out=bst[:, rp, SSQ, tt:tt + 1]),
                   reads=list(src_regs), writes=["sqs", ("bst", rp, tt)])
            t.emit("vector",
                   lambda e, s=src_ap, tt=tt, rp=rp: e.reduce_sum(
                       bst[:, rp, S1, tt:tt + 1], s, axis=AX.X),
                   reads=list(src_regs), writes=[("bst", rp, tt)])

        def ln_chain(rp=0):
            rd = [("bst", rp, i) for i in range(NT)]
            t.emit("vector",
                   lambda e, rp=rp: e.tensor_scalar_mul(
                       bst[:, rp, NEGM, :], bst[:, rp, S1, :], -1.0 / D),
                   reads=rd, writes=[("bch", rp)] + rd)
            t.emit("vector",
                   lambda e, rp=rp: e.tensor_mul(
                       bst[:, rp, M2, :], bst[:, rp, NEGM, :], bst[:, rp, NEGM, :]),
                   reads=[("bch", rp)], writes=[("bch", rp)])
            t.emit("vector",
                   lambda e, rp=rp: e.scalar_tensor_tensor(
                       out=bst[:, rp, VAR, :], in0=bst[:, rp, SSQ, :], scalar=1.0 / D,
                       in1=bst[:, rp, M2, :], op0=ALU.mult, op1=ALU.subtract),
                   reads=[("bch", rp)], writes=[("bch", rp)])
            t.emit("vector",
                   lambda e, rp=rp: e.tensor_scalar_add(
                       bst[:, rp, VAR, :], bst[:, rp, VAR, :], EPS),
                   reads=[("bch", rp)], writes=[("bch", rp)])
            t.emit("scalar",
                   lambda e, rp=rp: e.activation(
                       bst[:, rp, LNV, :], bst[:, rp, VAR, :], AF.Ln),
                   reads=[("bch", rp)], writes=[("bch", rp)])
            t.emit("scalar",
                   lambda e, rp=rp: e.activation(
                       bst[:, rp, RSTD, :], bst[:, rp, LNV, :], AF.Exp, scale=-0.5),
                   reads=[("bch", rp)], writes=[("bch", rp)])

        def ln_apply(src_ap, src_regs, out_ap, out_regs, tt, rp=0):
            t.emit("vector",
                   lambda e, s=src_ap, o=out_ap, tt=tt, rp=rp: e.tensor_scalar(
                       out=o, in0=s, scalar1=bst[:, rp, NEGM, tt:tt + 1],
                       scalar2=bst[:, rp, RSTD, tt:tt + 1], op0=ALU.add, op1=ALU.mult),
                   reads=list(src_regs) + [("bch", rp)], writes=list(out_regs))

        def stcol(par, i):
            return stat[:, par * 8 + i: par * 8 + i + 1]

        def ln_tile(src_ap, src_regs, out_ap, out_regs, par, extra_reads=()):
            sreg = ("stat", par)
            qreg = "sqs"
            t.emit("scalar",
                   lambda e, s=src_ap, a=stcol(par, SSQ):
                       e.activation(sqs[:], s, AF.Square, accum_out=a),
                   reads=list(src_regs) + list(extra_reads),
                   writes=[qreg, sreg])
            t.emit("vector",
                   lambda e, s=src_ap, o=stcol(par, S1): e.reduce_sum(o, s, axis=AX.X),
                   reads=list(src_regs), writes=[sreg])
            t.emit("vector",
                   lambda e: e.tensor_scalar_mul(stcol(par, NEGM), stcol(par, S1), -1.0 / D),
                   reads=[sreg], writes=[sreg])
            t.emit("vector",
                   lambda e: e.tensor_mul(stcol(par, M2), stcol(par, NEGM), stcol(par, NEGM)),
                   reads=[sreg], writes=[sreg])
            t.emit("vector",
                   lambda e: e.scalar_tensor_tensor(
                       out=stcol(par, VAR), in0=stcol(par, SSQ), scalar=1.0 / D,
                       in1=stcol(par, M2), op0=ALU.mult, op1=ALU.subtract),
                   reads=[sreg], writes=[sreg])
            t.emit("vector",
                   lambda e: e.tensor_scalar_add(stcol(par, VAR), stcol(par, VAR), EPS),
                   reads=[sreg], writes=[sreg])
            t.emit("scalar",
                   lambda e: e.activation(stcol(par, LNV), stcol(par, VAR), AF.Ln),
                   reads=[sreg], writes=[sreg])
            t.emit("scalar",
                   lambda e: e.activation(stcol(par, RSTD), stcol(par, LNV), AF.Exp, scale=-0.5),
                   reads=[sreg], writes=[sreg])
            t.emit("vector",
                   lambda e, s=src_ap, o=out_ap: e.tensor_scalar(
                       out=o, in0=s, scalar1=stcol(par, NEGM), scalar2=stcol(par, RSTD),
                       op0=ALU.add, op1=ALU.mult),
                   reads=list(src_regs) + [sreg], writes=list(out_regs))

        for r in range(4):
            t.emit("gpsimd", lambda e, r=r: e.memset(diagm[:, r, :], 1.0),
                   writes=[("dm", r)])
            t.emit("gpsimd",
                   lambda e, r=r: e.affine_select(
                       out=diagm[:, r, :], in_=diagm[:, r, :],
                       compare_op=ALU.is_ge, fill=0.0, base=-128 * r,
                       pattern=[[1, 512]], channel_multiplier=-1),
                   reads=[("dm", r)], writes=[("dm", r)])
        t.emit("sync", lambda e: e.dma_start(btab_sb[:], btab[:]), writes=["btab"], dma=True)
        t.emit("gpsimd", lambda e: e.memset(onec[:], 1.0), writes=["onec"])
        t.emit("gpsimd", lambda e: e.memset(oner[:], 1.0), writes=["oner"])
        t.emit("sync", lambda e: e.dma_start(kidx[:], kvidx[:]), writes=["kidx"], dma=True)
        t.emit("sync", lambda e: e.dma_start(gb[:], gateb[:]), writes=["gb"], dma=True)
        t.emit("gpsimd", lambda e: e.dma_start(gwt_sb[:], gwt[:]), writes=["gwt"], dma=True)
        t.emit("gpsimd",
               lambda e: e.dma_start(gwsg_sb[:], gwsg.rearrange("(c p) e -> p c e", p=P)),
               writes=["gwsg"], dma=True)


        lnfb = [at("lnfb0", [P, ND, 1024], BF16, 38848),
                at("lnfb1", [P, ND, 1024], BF16, 71616)]
        kwres = at("kwres", [P, ND, NH, HD], BF16, 104384)
        kev = at("kev", [P, 2, 1024], BF16, 169920)
        vev = at("vev", [P, 2, D], BF16, 174016)

        t.emit("gpsimd",
               lambda e: e.dma_start(
                   kwres[:], kwt.rearrange("(c p) (h d) -> p c h d", p=P, d=HD)),
               writes=["kwres"], dma=True)
        xst2 = at("xst2", [P, 2, D], F32, 182208)

        def k_head(R, h):
            rp = R % 2
            bb = ((R * NH + h) % 2) * 2
            for c in range(ND):
                for g in range(2):
                    t.emit("tensor",
                           lambda e, c=c, g=g, h=h, bb=bb, rp=rp: e.matmul(
                               pb[bb + g][:, 0:512],
                               kwres[:, c, h, :],
                               lnfb[rp][:, c, g * 512:(g + 1) * 512],
                               start=(c == 0), stop=(c == ND - 1)),
                           reads=["kwres", ("lnfb", rp)],
                           writes=[f"pb{bb + g}"],
                           pe_inc=(c == ND - 1))
            kp = h % 2
            for g in range(2):
                t.emit("scalar",
                       lambda e, g=g, bb=bb, kp=kp: e.activation(
                           kev[:, kp, g * 512:(g + 1) * 512], pb[bb + g][:, 0:512],
                           AF.Copy, scale=QSCALE),
                       reads=[f"pb{bb + g}"], writes=[("kev", kp)])
            t.emit("sync",
                   lambda e, h=h, R=R, kp=kp: e.dma_start(
                       kdr[h, :, R * 1024:(R + 1) * 1024], kev[:, kp, :]),
                   reads=[("kev", kp)], writes=[("kdr", h)], dma=True,
                   dsem=f"d_kdr{kp}")

        for s in range(6):
            for i in range(8):
                if s >= 2:
                    k_head(s - 2, 2 * i)
                    k_head(s - 2, 2 * i + 1)
                if s < 4:
                    g = s * 8 + i
                    par = i % 2
                    t.emit("sync",
                           lambda e, g=g, par=par: e.dma_start(
                               xst[:, par, :], xfull[g * P:(g + 1) * P, :]),
                           writes=[("xst", par)], dma=True)
                    ln_stats(xst[:, par, :], [("xst", par)], i, rp=s % 2)
                if 1 <= s <= 4:
                    R = s - 1
                    g = R * 8 + i
                    par = i % 2
                    t.emit("sync",
                           lambda e, g=g, par=par: e.dma_start(
                               xst2[:, par, :], xfull[g * P:(g + 1) * P, :]),
                           writes=[("xs2", par)], dma=True)
                    ln_apply(xst2[:, par, :], [("xs2", par)],
                             lntm[:, par, :], [("lntm", par)], i, rp=R % 2)
                    t.emit("sync",
                           lambda e, g=g, par=par: e.dma_start(
                               ln0st[g * P:(g + 1) * P, :], lntm[:, par, :]),
                           reads=[("lntm", par)], writes=[("ln0st", R)], dma=True,
                           dsem=f"d_ln0st{par}")
            if s < 4:
                ln_chain(rp=s % 2)
            if 1 <= s <= 4:
                R = s - 1
                rp = R % 2
                for c in range(ND):
                    t.emit("scalar",
                           lambda e, R=R, c=c, rp=rp: e.dma_start(
                               lnfb[rp][:, c, :],
                               ln0st[R * 1024:(R + 1) * 1024, c * P:(c + 1) * P],
                               transpose=True),
                           reads=[("ln0st", R)], writes=[("lnfb", rp)], dma=True)
                t.emit("sync",
                       lambda e, R=R, rp=rp: e.dma_start(
                           lnfm.rearrange("c p t -> p c t")[:, :, R * 1024:(R + 1) * 1024],
                           lnfb[rp][:]),
                       reads=[("lnfb", rp)], writes=[("lnfm", R)], dma=True,
                       dsem=f"d_lnfm{rp}")
        t.barrier()

        vw_ap = kwres[:].rearrange("p c h d -> p c (h d)")
        t.emit("gpsimd",
               lambda e: e.dma_start(
                   vw_ap, vwt.rearrange("(c p) e -> p c e", p=P)),
               writes=["kwres"], dma=True)
        ownfm = at("ownfm", [P, ND, 1024], BF16, 137152)

        def ownln_piece(R, tt):
            par = tt % 2
            if R == 0:
                t.emit("sync",
                       lambda e, tt=tt, par=par: e.dma_start(
                           xst[:, par, :], xown[tt * P:(tt + 1) * P, :]),
                       writes=[("xst", par)], dma=True)
                ln_stats(xst[:, par, :], [("xst", par)], tt, rp=0)
                if tt == NT - 1:
                    ln_chain(rp=0)
            elif R == 1:
                t.emit("sync",
                       lambda e, tt=tt, par=par: e.dma_start(
                           xst2[:, par, :], xown[tt * P:(tt + 1) * P, :]),
                       writes=[("xs2", par)], dma=True)
                ln_apply(xst2[:, par, :], [("xs2", par)],
                         lntm[:, par, :], [("lntm", par)], tt, rp=0)
                t.emit("sync",
                       lambda e, tt=tt, par=par: e.dma_start(
                           ownst[tt * P:(tt + 1) * P, :], lntm[:, par, :]),
                       reads=[("lntm", par)], writes=["ownst"], dma=True,
                       dsem=f"d_ownst{par}")

        for R in range(4):
            rp = R % 2
            t.emit("sync",
                   lambda e, R=R, rp=rp: e.dma_start(
                       lnfb[rp][:],
                       lnfm.rearrange("c p t -> p c t")[:, :, R * 1024:(R + 1) * 1024]),
                   reads=[("lnfm", R)], writes=[("lnfb", rp)], dma=True)
            for tt in range(8):
                T = R * 8 + tt
                bb = (T % 2) * 4
                vp = T % 2
                ownln_piece(R, tt)
                for c in range(ND):
                    for eg in range(4):
                        t.emit("tensor",
                               lambda e, c=c, eg=eg, tt=tt, bb=bb, rp=rp: e.matmul(
                                   pb[bb + eg][:, 0:512],
                                   lnfb[rp][:, c, tt * P:(tt + 1) * P],
                                   vw_ap[:, c, eg * 512:(eg + 1) * 512],
                                   start=(c == 0), stop=(c == ND - 1)),
                               reads=["kwres", ("lnfb", rp)],
                               writes=[f"pb{bb + eg}"],
                               pe_inc=(c == ND - 1))
                for eg in range(4):
                    t.emit("scalar",
                           lambda e, eg=eg, bb=bb, vp=vp: e.activation(
                               vev[:, vp, eg * 512:(eg + 1) * 512],
                               pb[bb + eg][:, 0:512], AF.Copy),
                           reads=[f"pb{bb + eg}"], writes=[("vev", vp)])
                t.emit("sync",
                       lambda e, T=T, vp=vp: e.dma_start(
                           vdr[T * P:(T + 1) * P, :], vev[:, vp, :]),
                       reads=[("vev", vp)], writes=["v0"], dma=True,
                       dsem=f"d_v0{vp}")
        t.barrier()

        q_fm = at("q_fm", [P, NH, 1024], BF16, 38848)
        gate_all = at("gate_all", [P, NH, 1024], BF16, 71616)
        qwres = at("qwres", [P, ND, 8, HD], BF16, 104384)

        for c in range(ND):
            t.emit("scalar",
                   lambda e, c=c: e.dma_start(
                       ownfm[:, c, :], ownst[:, c * P:(c + 1) * P], transpose=True),
                   reads=["ownst"], writes=["ownfm"], dma=True)
        for half in range(2):
            t.emit("gpsimd",
                   lambda e, half=half: e.dma_start(
                       qwres[:],
                       qwt.rearrange("(c p) (h d) -> p c h d", p=P, d=HD)[:, :, half * 8:(half + 1) * 8, :]),
                   writes=["qwres"], dma=True)
            for hh in range(8):
                h = half * 8 + hh
                bb = (h % 2) * 2
                for c in range(ND):
                    for g in range(2):
                        t.emit("tensor",
                               lambda e, c=c, g=g, hh=hh, bb=bb: e.matmul(
                                   pb[bb + g][:, 0:512],
                                   qwres[:, c, hh, :],
                                   ownfm[:, c, g * 512:(g + 1) * 512],
                                   start=(c == 0), stop=(c == ND - 1)),
                               reads=["qwres", "ownfm"],
                               writes=[f"pb{bb + g}"],
                               pe_inc=(c == ND - 1))
                for g in range(2):
                    t.emit("scalar",
                           lambda e, h=h, g=g, bb=bb: e.activation(
                               q_fm[:, h, g * 512:(g + 1) * 512],
                               pb[bb + g][:, 0:512], AF.Copy),
                           reads=[f"pb{bb + g}"], writes=["qfm"])
        for h in range(NH):
            bb = (h % 2) * 2
            for g in range(2):
                t.emit("tensor",
                       lambda e, h=h, g=g, bb=bb: e.matmul(
                           pb[bb + g][:, 0:512], gwt_sb[:],
                           q_fm[:, h, g * 512:(g + 1) * 512],
                           start=True, stop=True),
                       reads=["gwt", "qfm"], writes=[f"pb{bb + g}"], pe_inc=True)
                t.emit("scalar",
                       lambda e, h=h, g=g, bb=bb: e.activation(
                           gate_all[:, h, g * 512:(g + 1) * 512],
                           pb[bb + g][:, 0:512], AF.Sigmoid, bias=gb[:]),
                       reads=[f"pb{bb + g}", "gb"], writes=["gate"])
        t.barrier()

        gav = at("gav", [P, NH, 1024], BF16, 104384)
        kbulk = [at("kbulk0", [P, NKB * P], BF16, 137152),
                 at("kbulk1", [P, NKB * P], BF16, 144320)]
        vbulk = [at("vbulk0", [P, NKB, P], BF16, 151488),
                 at("vbulk1", [P, NKB, P], BF16, 158656)]
        kdg = [at("kdg0", [P, 512], BF16, 165824),
               at("kdg1", [P, 512], BF16, 166848)]
        vdg = [at("vdg0", [P, 4, P], BF16, 167872),
               at("vdg1", [P, 4, P], BF16, 168896)]
        expt = [at("expt0", [P, 512], BF16, 169920),
                at("expt1", [P, 512], BF16, 170944),
                at("expt2", [P, 512], BF16, 174016),
                at("expt3", [P, 512], BF16, 175040)]
        avsb = [at("avsb0", [P, 512], BF16, 171968),
                at("avsb1", [P, 512], BF16, 172992)]

        kdr_blocks = kdr.rearrange("h p (b t) -> (h p b) t", t=P)
        job = 0
        pend = [None]
        for g in range(2):
            n = NKA if g == 0 else NKB
            boff = 0 if g == 0 else NKA
            for h in range(NH):
                jp = job % 2
                avb = 4 + (job % 2)
                zb = 6 + (job % 2)
                t.emit("sync",
                       lambda e, h=h, n=n, jp=jp: e.dma_start(
                           kbulk[jp][:, 0:n * P], kdr[h, :, 0:n * P]),
                       reads=[("kdr", h)], writes=[("kbulk", jp)], dma=True)
                t.emit("sync",
                       lambda e, h=h, n=n, jp=jp: e.dma_start(
                           vbulk[jp][:, 0:n, :],
                           vdr.rearrange("(b p) e -> p b e", p=P)[:, 0:n, h * HD:(h + 1) * HD]),
                       reads=["v0"], writes=[("vbulk", jp)], dma=True)
                t.emit("gpsimd",
                       lambda e, h=h, g=g, jp=jp: e.indirect_dma_start(
                           kdg[jp][:], None, kdr_blocks[:, :],
                           IndirectOffsetOnAxis(kidx[:, g:g + 1], 0),
                           element_offset=h * P * S),
                       reads=[("kdr", h), "kidx"], writes=[("kdg", jp)], dma=True)
                for i in range(4):
                    t.emit("gpsimd",
                           lambda e, h=h, g=g, i=i, jp=jp: e.indirect_dma_start(
                               vdg[jp][:, i, :], None, vdr[:, :],
                               IndirectOffsetOnAxis(kidx[:, 2 + g * 4 + i: 3 + g * 4 + i], 0),
                               element_offset=h * HD),
                           reads=["v0", "kidx"], writes=[("vdg", jp)], dma=True)
                qslice = q_fm[:, h, g * 512:(g + 1) * 512]

                def sc_stage(s):
                    sp = s % 4
                    diag = s >= n
                    i = s - n
                    if not diag:
                        klhs = (lambda e, s=s, jp=jp: kbulk[jp][:, s * P:(s + 1) * P])
                        kreads = [("kbulk", jp)]
                    else:
                        klhs = (lambda e, i=i, jp=jp: kdg[jp][:, i * P:(i + 1) * P])
                        kreads = [("kdg", jp)]
                    t.emit("tensor",
                           lambda e, klhs=klhs, qs=qslice, sp=sp: e.matmul(
                               pb[sp][:, 0:512], klhs(e), qs,
                               start=True, stop=True),
                           reads=kreads + ["qfm"], writes=[f"pb{sp}"],
                           pe_inc=True)
                    bcol = 40 if diag else boff + s
                    t.emit("scalar",
                           lambda e, sp=sp, bcol=bcol: e.activation(
                               expt[sp][:], pb[sp][:, 0:512], AF.Exp,
                               bias=btab_sb[:, bcol:bcol + 1]),
                           reads=[f"pb{sp}", "btab"], writes=[("expt", sp)])
                    if diag:
                        t.emit("vector",
                               lambda e, sp=sp, i=i: e.tensor_mul(
                                   expt[sp][:], expt[sp][:], diagm[:, i, :]),
                               reads=[("expt", sp), ("dm", i)], writes=[("expt", sp)])

                def av_stage(s):
                    sp = s % 4
                    diag = s >= n
                    i = s - n
                    if not diag:
                        vlhs = (lambda e, s=s, jp=jp: vbulk[jp][:, s, :])
                        vreads = [("vbulk", jp)]
                    else:
                        vlhs = (lambda e, i=i, jp=jp: vdg[jp][:, i, :])
                        vreads = [("vdg", jp)]
                    t.emit("tensor",
                           lambda e, vlhs=vlhs, sp=sp, avb=avb, s=s, last=(s == n + 3): e.matmul(
                               pb[avb][:, 0:512], vlhs(e), expt[sp][:],
                               start=(s == 0), stop=last),
                           reads=vreads + [("expt", sp)], writes=[f"pb{avb}"],
                           pe_inc=(s == n + 3))
                    t.emit("tensor",
                           lambda e, sp=sp, zb=zb, s=s, last=(s == n + 3): e.matmul(
                               pb[zb][0:1, 0:512], onec[:], expt[sp][:],
                               start=(s == 0), stop=last),
                           reads=["onec", ("expt", sp)], writes=[f"pb{zb}"],
                           pe_inc=(s == n + 3))

                LAG = 2
                for s in range(n + 4):
                    sc_stage(s)
                    if s == LAG and pend[0] is not None:
                        pend[0]()
                        pend[0] = None
                    if s >= LAG:
                        av_stage(s - LAG)
                for s in range(n + 4 - LAG, n + 4):
                    av_stage(s)

                t.emit("scalar",
                       lambda e, zb=zb, jp=jp: e.activation(
                           zr[:, jp, :], pb[zb][0:1, 0:512], AF.Ln),
                       reads=[f"pb{zb}"], writes=[("zr", jp)])
                t.emit("scalar",
                       lambda e, jp=jp: e.activation(
                           zrb[:, jp, :], zr[:, jp, :], AF.Exp, scale=-1.0),
                       reads=[("zr", jp)], writes=[("zrb", jp)])

                def tail(h=h, g=g, jp=jp, avb=avb):
                    t.emit("scalar",
                           lambda e: e.activation(
                               avsb[jp][:], pb[avb][:, 0:512], AF.Copy),
                           reads=[f"pb{avb}"], writes=[("avsb", jp)])
                    t.emit("tensor",
                           lambda e: e.matmul(
                               pb[avb][:, 0:512], oner[:], zrb[:, jp, :],
                               start=True, stop=True),
                           reads=["oner", ("zrb", jp), ("avsb", jp)],
                           writes=[f"pb{avb}"], pe_inc=True)
                    t.emit("vector",
                           lambda e: e.tensor_mul(
                               avsb[jp][:], avsb[jp][:],
                               gate_all[:, h, g * 512:(g + 1) * 512]),
                           reads=[("avsb", jp), "gate"], writes=[("avsb", jp)])
                    t.emit("vector",
                           lambda e: e.tensor_mul(
                               gav[:, h, g * 512:(g + 1) * 512], avsb[jp][:],
                               pb[avb][:, 0:512]),
                           reads=[("avsb", jp), f"pb{avb}"], writes=["gav"])
                pend[0] = tail
                job += 1
        pend[0]()
        pend[0] = None
        t.barrier()

        lnf1 = at("lnf1", [P, ND, 1024], BF16, 38848)
        owres = at("owres", [P, ND, D], BF16, 137152)
        aosb = at("aosb", [P, 2, D], F32, 71616)
        ansb = at("ansb", [P, D], F32, 88000)

        t.emit("gpsimd",
               lambda e: e.dma_start(owres[:], owt.rearrange("(c p) e -> p c e", p=P)),
               writes=["owres"], dma=True)
        def f_mms(tt):
            bb = (tt % 2) * 4
            ap = tt % 2
            for c in range(ND):
                for eg in range(4):
                    t.emit("tensor",
                           lambda e, c=c, eg=eg, tt=tt, bb=bb: e.matmul(
                               pb[bb + eg][:, 0:512],
                               gav[:, c, tt * P:(tt + 1) * P],
                               owres[:, c, eg * 512:(eg + 1) * 512],
                               start=(c == 0), stop=(c == ND - 1)),
                           reads=["gav", "owres"], writes=[f"pb{bb + eg}"],
                           pe_inc=(c == ND - 1))
            for eg in range(4):
                t.emit("scalar",
                       lambda e, eg=eg, bb=bb, ap=ap: e.activation(
                           aosb[:, ap, eg * 512:(eg + 1) * 512], pb[bb + eg][:, 0:512],
                           AF.Copy),
                       reads=[f"pb{bb + eg}"], writes=[("aosb", ap)])

        def f_epi(tt):
            ap = tt % 2
            par = tt % 2
            ln_tile(aosb[:, ap, :], [("aosb", ap)], ansb[:], ["ansb"], par)
            t.emit("sync",
                   lambda e, tt=tt, par=par: e.dma_start(
                       xst[:, par, :], xown[tt * P:(tt + 1) * P, :]),
                   writes=[("xst", par)], dma=True)
            t.emit("vector",
                   lambda e, par=par: e.tensor_add(ansb[:], ansb[:], xst[:, par, :]),
                   reads=["ansb", ("xst", par)], writes=["ansb"])
            t.emit("sync",
                   lambda e, tt=tt: e.dma_start(x1dr[tt * P:(tt + 1) * P, :], ansb[:]),
                   reads=["ansb"], writes=["x1dr"], dma=True)
            ln_tile(ansb[:], ["ansb"], lntm[:, par, :], [("lntm", par)], par)
            t.emit("sync",
                   lambda e, tt=tt, par=par: e.dma_start(
                       ln1st[tt * P:(tt + 1) * P, :], lntm[:, par, :]),
                   reads=[("lntm", par)], writes=["ln1st"], dma=True,
                   dsem=f"d_ln1st{par}")

        for tt in range(NT):
            f_mms(tt)
            if tt >= 1:
                f_epi(tt - 1)
        f_epi(NT - 1)
        for c in range(ND):
            t.emit("scalar",
                   lambda e, c=c: e.dma_start(
                       lnf1[:, c, :], ln1st[:, c * P:(c + 1) * P], transpose=True),
                   reads=["ln1st"], writes=["lnf1"], dma=True)
        t.barrier()

        for tt in range(NT):
            bk = tt % 2
            for c in range(ND):
                t.emit("tensor",
                       lambda e, c=c, tt=tt, bk=bk: e.matmul(
                           pb[bk][:, 0:17],
                           lnf1[:, c, tt * P:(tt + 1) * P], gwsg_sb[:, c, :],
                           start=(c == 0), stop=(c == ND - 1)),
                       reads=["lnf1", "gwsg"], writes=[f"pb{bk}"],
                       pe_inc=(c == ND - 1))
            t.emit("scalar",
                   lambda e, bk=bk: e.activation(rsb[:], pb[bk][:, 0:17], AF.Copy),
                   reads=[f"pb{bk}"], writes=["rsb"])
            t.emit("vector", lambda e: e.reduce_max(stat[:, 0:1], rsb[:, 0:16], axis=AX.X),
                   reads=["rsb"], writes=[("stat", 0)])
            t.emit("vector", lambda e: e.tensor_scalar_mul(stat[:, 1:2], stat[:, 0:1], -1.0),
                   reads=[("stat", 0)], writes=[("stat", 0)])
            t.emit("scalar",
                   lambda e: e.activation(r16[:], rsb[:, 0:16], AF.Exp,
                                          bias=stat[:, 1:2], accum_out=stat[:, 2:3]),
                   reads=["rsb", ("stat", 0)], writes=["r16", ("stat", 0)])
            t.emit("vector", lambda e: e.reciprocal(stat[:, 3:4], stat[:, 2:3]),
                   reads=[("stat", 0)], writes=[("stat", 0)])
            t.emit("vector",
                   lambda e: e.tensor_scalar_mul(r16[:], r16[:], stat[:, 3:4]),
                   reads=["r16", ("stat", 0)], writes=["r16"])
            t.emit("vector", lambda e: e.max(out=mx8[:], in_=r16[:]),
                   reads=["r16"], writes=["mx8"])
            t.emit("vector", lambda e: e.memset(mx8[:, 4:8], 0.0),
                   reads=["mx8"], writes=["mx8"])
            t.emit("vector",
                   lambda e: e.match_replace(out=rsb[:, 0:16], in_to_replace=mx8[:],
                                             in_values=r16[:], imm_value=0.0),
                   reads=["mx8", "r16"], writes=["rsb"])
            t.emit("vector",
                   lambda e, tt=tt: e.tensor_sub(cw[:, tt, :], r16[:], rsb[:, 0:16]),
                   reads=["r16", "rsb"], writes=["cw"])
            t.emit("scalar",
                   lambda e: e.activation(stat[:, 4:5], rsb[:, 16:17], AF.Exp, scale=-1.0),
                   reads=["rsb"], writes=[("stat", 0)])
            t.emit("vector",
                   lambda e: e.tensor_scalar_add(stat[:, 4:5], stat[:, 4:5], 1.0),
                   reads=[("stat", 0)], writes=[("stat", 0)])
            t.emit("vector",
                   lambda e, tt=tt: e.reciprocal(sg[:, tt:tt + 1], stat[:, 4:5]),
                   reads=[("stat", 0)], writes=["sg"])
        t.barrier()

        acc = at("acc", [P, NT, D], F32, 71616)
        he = at("he", [P, NIE, 1024], BF16, 137152)
        w1r = [at("w1r0", [P, ND, P], BF16, 159680),
               at("w1r1", [P, ND, P], BF16, 163776)]
        w3r = [at("w3r0", [P, ND, P], BF16, 167872),
               at("w3r1", [P, ND, P], BF16, 171968)]
        w2p = [at("w2p0", [P, NIE, 512], BF16, 176064),
               at("w2p1", [P, NIE, 512], BF16, 187328)]
        ssc = at("ssc", [P, 2, 512], F32, 198592)

        def expert(widx, first):
            kind, ix = widx
            if kind == "s":
                w1src = sw1.rearrange("(c p) i -> p c i", p=P)
                w3src = sw3.rearrange("(c p) i -> p c i", p=P)
                w2src = sw2.rearrange("(c p) e -> p c e", p=P)
                icoff = ix * NIE
            else:
                w1src = w1[ix * D:(ix + 1) * D, :].rearrange("(c p) i -> p c i", p=P)
                w3src = w3[ix * D:(ix + 1) * D, :].rearrange("(c p) i -> p c i", p=P)
                w2src = w2[ix * 1408:(ix + 1) * 1408, :].rearrange("(c p) e -> p c e", p=P)
                icoff = 0
            for ic in range(NIE):
                wp = ic % 2
                t.emit("gpsimd",
                       lambda e, ic=ic, wp=wp, w1src=w1src: e.dma_start(
                           w1r[wp][:],
                           w1src[:, :, (icoff + ic) * P:(icoff + ic + 1) * P]),
                       writes=[("w1r", wp)], dma=True)
                t.emit("gpsimd",
                       lambda e, ic=ic, wp=wp, w3src=w3src: e.dma_start(
                           w3r[wp][:],
                           w3src[:, :, (icoff + ic) * P:(icoff + ic + 1) * P]),
                       writes=[("w3r", wp)], dma=True)
                bb = (ic % 2) * 4
                for c in range(ND):
                    for g in range(2):
                        t.emit("tensor",
                               lambda e, c=c, g=g, wp=wp, bb=bb: e.matmul(
                                   pb[bb + g][:, 0:512], w1r[wp][:, c, :],
                                   lnf1[:, c, g * 512:(g + 1) * 512],
                                   start=(c == 0), stop=(c == ND - 1)),
                               reads=[("w1r", wp), "lnf1"], writes=[f"pb{bb + g}"],
                               pe_inc=(c == ND - 1))
                for c in range(ND):
                    for g in range(2):
                        t.emit("tensor",
                               lambda e, c=c, g=g, wp=wp, bb=bb: e.matmul(
                                   pb[bb + 2 + g][:, 0:512], w3r[wp][:, c, :],
                                   lnf1[:, c, g * 512:(g + 1) * 512],
                                   start=(c == 0), stop=(c == ND - 1)),
                               reads=[("w3r", wp), "lnf1"], writes=[f"pb{bb + 2 + g}"],
                               pe_inc=(c == ND - 1))
                for g in range(2):
                    t.emit("scalar",
                           lambda e, g=g, bb=bb: e.activation(
                               ssc[:, g, :], pb[bb + g][:, 0:512], AF.Silu),
                           reads=[f"pb{bb + g}"], writes=[("ssc", g)])
                    t.emit("vector",
                           lambda e, g=g, ic=ic, bb=bb: e.tensor_mul(
                               he[:, ic, g * 512:(g + 1) * 512], ssc[:, g, :],
                               pb[bb + 2 + g][:, 0:512]),
                           reads=[("ssc", g), f"pb{bb + 2 + g}"],
                           writes=[("he", ic)])
            for eg in range(4):
                wp = eg % 2
                t.emit("gpsimd",
                       lambda e, eg=eg, wp=wp, w2src=w2src: e.dma_start(
                           w2p[wp][:],
                           w2src[:, icoff:icoff + NIE, eg * 512:(eg + 1) * 512]),
                       writes=[("w2p", wp)], dma=True)
                for tp in range(4):
                    bb = (tp % 2) * 2
                    for ic in range(NIE):
                        for i in range(2):
                            t.emit("tensor",
                                   lambda e, ic=ic, i=i, tp=tp, wp=wp, bb=bb: e.matmul(
                                       pb[bb + i][:, 0:512],
                                       he[:, ic, (tp * 2 + i) * P:(tp * 2 + i + 1) * P],
                                       w2p[wp][:, ic, :],
                                       start=(ic == 0), stop=(ic == NIE - 1)),
                                   reads=[("he", ic), ("w2p", wp)],
                                   writes=[f"pb{bb + i}"],
                                   pe_inc=(ic == NIE - 1))
                    for i in range(2):
                        tt = tp * 2 + i
                        scal = (lambda e, tt=tt: sg[:, tt:tt + 1]) if kind == "s" \
                            else (lambda e, tt=tt, ix=ix: cw[:, tt, ix:ix + 1])
                        sreads = ["sg"] if kind == "s" else ["cw"]
                        if first:
                            t.emit("vector",
                                   lambda e, tt=tt, eg=eg, i=i, bb=bb, scal=scal:
                                       e.tensor_scalar_mul(
                                           acc[:, tt, eg * 512:(eg + 1) * 512],
                                           pb[bb + i][:, 0:512], scal(e)),
                                   reads=[f"pb{bb + i}"] + sreads,
                                   writes=[("acc", tt)])
                        else:
                            t.emit("vector",
                                   lambda e, tt=tt, eg=eg, i=i, bb=bb, scal=scal:
                                       e.scalar_tensor_tensor(
                                           out=acc[:, tt, eg * 512:(eg + 1) * 512],
                                           in0=pb[bb + i][:, 0:512], scalar=scal(e),
                                           in1=acc[:, tt, eg * 512:(eg + 1) * 512],
                                           op0=ALU.mult, op1=ALU.add),
                                   reads=[f"pb{bb + i}", ("acc", tt)] + sreads,
                                   writes=[("acc", tt)])

        for q in range(4):
            expert(("s", q), first=(q == 0))
        for e_ in range(NE):
            expert(("e", e_), first=False)
        t.barrier()

        jns = at("jns", [P, 2, D], F32, 137152)
        for tt in range(NT):
            ln_stats(acc[:, tt, :], [("acc", tt)], tt)
        ln_chain()
        for tt in range(NT):
            par = tt % 2
            ln_apply(acc[:, tt, :], [("acc", tt)], jns[:, par, :], [("jns", par)], tt)
            t.emit("sync",
                   lambda e, tt=tt, par=par: e.dma_start(
                       xst[:, par, :], x1dr[tt * P:(tt + 1) * P, :]),
                   reads=["x1dr"], writes=[("xst", par)], dma=True)
            t.emit("vector",
                   lambda e, par=par: e.tensor_add(
                       jns[:, par, :], jns[:, par, :], xst[:, par, :]),
                   reads=[("jns", par), ("xst", par)], writes=[("jns", par)])
            t.emit("sync",
                   lambda e, tt=tt, par=par: e.dma_start(
                       out[tt * P:(tt + 1) * P, :], jns[:, par, :]),
                   reads=[("jns", par)], writes=["outw"], dma=True,
                   dsem=f"d_out{par}")
        t.final_waits()

        block = stack.enter_context(nc.Block())
        t.replay(block)
    return nc


def kernel(L0_attn_x, L0_attn_q_w, L0_attn_k_w, L0_attn_v_w, L0_attn_o_w,
           L0_attn_gate_w, L0_attn_gate_b, L0_ffn_gate_w, L0_ffn_w1, L0_ffn_w3,
           L0_ffn_w2, L0_ffn_s_w1, L0_ffn_s_w3, L0_ffn_s_w2, L0_ffn_sg_w):
    x = np.ascontiguousarray(np.asarray(L0_attn_x, dtype=np.float32))
    B = x.shape[0]
    shared = {
        "qwt": np.ascontiguousarray(np.asarray(L0_attn_q_w, np.float32).T),
        "kwt": np.ascontiguousarray(np.asarray(L0_attn_k_w, np.float32).T),
        "vwt": np.ascontiguousarray(np.asarray(L0_attn_v_w, np.float32).T),
        "owt": np.ascontiguousarray(np.asarray(L0_attn_o_w, np.float32).T),
        "gwt": np.ascontiguousarray(np.asarray(L0_attn_gate_w, np.float32).T),
        "gateb": np.ascontiguousarray(np.asarray(L0_attn_gate_b, np.float32).reshape(P, 1)),
        "gwsg": np.ascontiguousarray(np.concatenate(
            [np.asarray(L0_ffn_gate_w, np.float32),
             np.asarray(L0_ffn_sg_w, np.float32)], axis=1)),
        "sw1": np.ascontiguousarray(np.asarray(L0_ffn_s_w1, np.float32)),
        "sw3": np.ascontiguousarray(np.asarray(L0_ffn_s_w3, np.float32)),
        "sw2": np.ascontiguousarray(np.asarray(L0_ffn_s_w2, np.float32)),
        "w1": np.ascontiguousarray(np.asarray(L0_ffn_w1, np.float32).reshape(NE * D, 1408)),
        "w3": np.ascontiguousarray(np.asarray(L0_ffn_w3, np.float32).reshape(NE * D, 1408)),
        "w2": np.ascontiguousarray(np.asarray(L0_ffn_w2, np.float32).reshape(NE * 1408, D)),
    }

    in_maps = []
    for c in range(8):
        b, j = c // 4, c % 4
        a, bb = j, 7 - j
        xown = np.ascontiguousarray(np.concatenate(
            [x[b, 512 * a:512 * (a + 1)], x[b, 512 * bb:512 * (bb + 1)]], axis=0))
        btab = np.full((P, 48), -1e30, np.float32)
        for s in range(NKA):
            if s < 4 * a:
                btab[:, s] = 14.0
        for s in range(NKB):
            if s < 4 * bb:
                btab[:, NKA + s] = 14.0
        btab[:, 40] = 14.0
        kvidx = np.zeros((P, 10), np.int32)
        pr = np.arange(P, dtype=np.int32)
        kvidx[:, 0] = pr * 32 + 4 * a
        kvidx[:, 1] = pr * 32 + 4 * bb
        for i in range(4):
            kvidx[:, 2 + i] = (4 * a + i) * P + pr
            kvidx[:, 6 + i] = (4 * bb + i) * P + pr
        m = {"xfull": x[b], "xown": xown,
             "btab": btab, "kvidx": kvidx}
        m.update(shared)
        in_maps.append(m)

    nc = build_program()
    res = run_bass_kernel_spmd(nc, in_maps, core_ids=list(range(8)))

    y = np.empty((B, S, D), np.float32)
    for c in range(8):
        b, j = c // 4, c % 4
        a, bb = j, 7 - j
        o = res.results[c]["out"]
        y[b, 512 * a:512 * (a + 1)] = o[0:512]
        y[b, 512 * bb:512 * (bb + 1)] = o[512:1024]
    return y
